# version 07
# baseline (speedup 1.0000x reference)
import numpy as np
import ml_dtypes
import concourse.bass as bass
import concourse.mybir as mybir
from concourse.bass_utils import run_bass_kernel_spmd

F32 = mybir.dt.float32
BF16 = mybir.dt.bfloat16
AF = mybir.ActivationFunctionType
ALU = mybir.AluOpType
AX = mybir.AxisListType

NCORES = 8
T = 1024
H = 2048
QSC = 0.2973017787506803

ENGS = ("t", "v", "s", "y", "g")

from collections import defaultdict


class B:
    def __init__(self, nc):
        self.nc = nc
        self.q = {e: [] for e in ENGS}
        self.cnt = defaultdict(int)
        self.lw = {e: defaultdict(int) for e in ENGS}
        self.sems = {}
        self.bank = [None] * 8

    def sem(self, name):
        if name not in self.sems:
            self.sems[name] = self.nc.semaphore("sem_" + name).__enter__()
        return self.sems[name]

    def emit(self, eng, mk, waits=None, inc=None, inc_by=1):
        acts = []
        for w in (waits or []):
            if w is None:
                continue
            s, v = w
            if v > 0 and self.lw[eng][s] < v:
                self.lw[eng][s] = v
                acts.append((s, v))
        ev = None
        if inc is not None:
            self.sem(inc)
            self.cnt[inc] += inc_by
            ev = (inc, self.cnt[inc])

        def fn(e, acts=acts, mk=mk, inc=inc, inc_by=inc_by):
            for (s, v) in acts:
                e.wait_ge(self.sems[s], v)
            ins = mk(e)
            if inc is not None:
                ins.then_inc(self.sems[inc], inc_by)
        self.q[eng].append(fn)
        return ev

    def dma(self, eng, out, in_, waits=None, sem=None):
        if sem is None:
            sem = "dy" if eng == "y" else "dg"
        return self.emit(eng, lambda e, o=out, i=in_: e.dma_start(out=o, in_=i),
                         waits=waits, inc=sem, inc_by=16)

    def finalize(self, block):
        for name, attr in (("t", "tensor"), ("v", "vector"), ("s", "scalar"),
                           ("y", "sync"), ("g", "gpsimd")):
            def body(e, name=name):
                for fn in self.q[name]:
                    fn(e)
            getattr(block, attr)(body)


def _hadamard(n):
    h = np.array([[1.0]], np.float32)
    while h.shape[0] < n:
        h = np.block([[h, h], [h, -h]])
    return h


def make_consts():
    conf = np.zeros((128, 1312), np.float32)
    conf[:, 0:128] = np.eye(128, dtype=np.float32)
    conf[0, 128:256] = 1.0
    conf[0, 1280] = 1e-5
    for e in range(8):
        conf[e, 256 + e * 128:256 + (e + 1) * 128] = 1.0
    conb = np.zeros((128, 2177), np.float32)
    for kt in range(4):
        q = np.arange(512)[None, :]
        p = np.arange(128)[:, None]
        conb[:, kt * 512:(kt + 1) * 512] = (q >= kt * 128 + p)
    conb[:, 2048:2049] = 1.0
    conb[:, 2049:2177] = _hadamard(128)
    return conf, conb.astype(ml_dtypes.bfloat16)


def decl_common(nc, b):
    ctx = {"nc": nc, "b": b}
    ctx["AB"] = nc.sbuf_tensor("AB", [128, 42 * 1024], F32).__enter__()
    ctx["ABH"] = ctx["AB"].bitcast(BF16)
    ctx["ROWS"] = nc.sbuf_tensor("ROWS", [1, 4 * 1024], F32).__enter__()
    ctx["SMALL"] = nc.sbuf_tensor("SMALL", [128, 80], F32).__enter__()
    ctx["CONF"] = nc.sbuf_tensor("CONF", [128, 1312], F32).__enter__()
    ctx["CONB"] = nc.sbuf_tensor("CONB", [128, 2177], BF16).__enter__()
    ctx["CONRT"] = nc.sbuf_tensor("CONRT", [128, 160], BF16).__enter__()
    ctx["C_T"] = nc.sbuf_tensor("C_T", [8, 1024], F32).__enter__()
    ctx["PS"] = [nc.psum_tensor(f"PS{i}", [128, 512], F32).__enter__() for i in range(8)]
    return ctx


def cbf(ctx, u0, u1):
    return ctx["ABH"][:, u0 * 1024:u1 * 1024]


def cf32(ctx, u0, u1):
    assert u0 % 2 == 0 and u1 % 2 == 0
    return ctx["AB"][:, (u0 // 2) * 1024:(u1 // 2) * 1024]


def eps_ap(ctx):
    return ctx["CONF"][0:1, 1280:1281]


def ident(ctx):
    return ctx["CONF"][:, 0:128]


def ones_row(ctx):
    return ctx["CONF"][0:1, 128:256]


def sel8(ctx, e):
    return ctx["CONF"][0:8, 256 + e * 128:256 + (e + 1) * 128]


def ones_bf(ctx):
    return ctx["CONB"][:, 2048:2049]


def h128c(ctx):
    return ctx["CONB"][:, 2049:2177]


def dmask(ctx, kt):
    return ctx["CONB"][:, kt * 512:(kt + 1) * 512]


def emit_load_consts(b, ctx, conf_in, conb_in):
    b.dma("y", ctx["CONF"][:, :], conf_in, sem="dcon")
    e2 = b.dma("y", ctx["CONB"][:, :], conb_in, sem="dcon")
    return [e2]


def mm(b, bank_ps, bank_idx, mk, waits, start, inc=None):
    w = list(waits)
    if start and b.bank[bank_idx] is not None:
        w.append(b.bank[bank_idx])
    return b.emit("t", mk, waits=w, inc=inc)


def evac(b, eng, bank_idx, mk, waits, inc):
    ev = b.emit(eng, mk, waits=waits, inc=inc)
    b.bank[bank_idx] = ev
    return ev


def emit_ln(b, ctx, ACC, XLN, xb, deps):
    PS = ctx["PS"]
    ROWS = ctx["ROWS"]
    s_row = ROWS[0:1, 0:1024]
    m_row = s_row
    q_row = ROWS[0:1, 1024:2048]
    ex2 = q_row
    m2 = ROWS[0:1, 2048:3072]
    b_row = m2
    r_row = ROWS[0:1, 3072:4096]
    RB = cf32(ctx, 68, 70)
    BB = cf32(ctx, 70, 72)

    A = lambda c: ACC[:, c * 1024:(c + 1) * 1024]
    XL = lambda c: XLN[:, c * 1024:(c + 1) * 1024]
    XB = lambda c: xb[:, (c % 4) * 1024:((c % 4) + 1) * 1024]

    xb_ev = [None] * 16
    sq_ev = [None] * 16
    mm_ev = [None] * 16
    for c in range(16):
        war = [mm_ev[c - 4]] if c >= 4 else []
        xb_ev[c] = b.emit("v", lambda e, c=c: e.tensor_copy(XB(c), A(c)),
                          waits=deps + war, inc="v")
        sq_ev[c] = b.emit("v", lambda e, c=c: e.tensor_tensor(XL(c), XB(c), XB(c), ALU.mult),
                          waits=[xb_ev[c]], inc="v")
        for hh in range(2):
            mm(b, PS, hh, lambda e, c=c, hh=hh: e.matmul(
                PS[hh][0:1, :], ones_bf(ctx), XB(c)[:, hh * 512:(hh + 1) * 512],
                start=(c == 0), stop=(c == 15)), [xb_ev[c]], start=(c == 0))
        for hh in range(2):
            mm_ev[c] = mm(b, PS, 2 + hh, lambda e, c=c, hh=hh: e.matmul(
                PS[2 + hh][0:1, :], ones_bf(ctx), XL(c)[:, hh * 512:(hh + 1) * 512],
                start=(c == 0), stop=(c == 15)), [sq_ev[c]], start=(c == 0),
                inc="t")
    mm_all = mm_ev[15]
    e1 = evac(b, "v", 0, lambda e: e.tensor_copy(s_row[:, 0:512], PS[0][0:1, :]), [mm_all], "v")
    e2 = evac(b, "v", 1, lambda e: e.tensor_copy(s_row[:, 512:1024], PS[1][0:1, :]), [mm_all], "v")
    e3 = evac(b, "v", 2, lambda e: e.tensor_copy(q_row[:, 0:512], PS[2][0:1, :]), [mm_all], "v")
    e4 = evac(b, "v", 3, lambda e: e.tensor_copy(q_row[:, 512:1024], PS[3][0:1, :]), [mm_all], "v")
    ev = b.emit("v", lambda e: e.tensor_scalar_mul(m_row, s_row, 1.0 / H), waits=[e1, e2], inc="v")
    ev2 = b.emit("v", lambda e: e.tensor_scalar_mul(ex2, q_row, 1.0 / H), waits=[e3, e4], inc="v")
    ev3 = b.emit("v", lambda e: e.tensor_mul(m2, m_row, m_row), waits=[ev], inc="v")
    var_ev = b.emit("v", lambda e: e.tensor_sub(ex2, ex2, m2), waits=[ev2, ev3], inc="v")
    ln_e = b.emit("s", lambda e: e.activation(r_row, ex2, AF.Ln, bias=eps_ap(ctx)), waits=[var_ev], inc="s")
    exp_e = b.emit("s", lambda e: e.activation(r_row, r_row, AF.Exp, scale=-0.5), waits=[ln_e], inc="s")
    br_ev = b.emit("v", lambda e: e.scalar_tensor_tensor(b_row, m_row, -1.0, r_row, ALU.mult, ALU.mult),
                   waits=[exp_e], inc="v")
    bc = []
    for (row, dst, base, wev) in ((r_row, RB, 4, exp_e), (b_row, BB, 6, br_ev)):
        for hh in range(2):
            tev = mm(b, PS, base + hh, lambda e, row=row, base=base, hh=hh: e.matmul(
                PS[base + hh][:, :], ones_row(ctx), row[:, hh * 512:(hh + 1) * 512],
                start=True, stop=True), [wev], start=True, inc="t")
            bc.append(evac(b, "v", base + hh, lambda e, dst=dst, base=base, hh=hh: e.tensor_copy(
                dst[:, hh * 512:(hh + 1) * 512], PS[base + hh][:, :]), [tev], "v"))
    out_ev = None
    apply_evs = []
    for c in range(16):
        ev1 = b.emit("v", lambda e, c=c: e.tensor_tensor(XL(c), A(c), RB, ALU.mult),
                     waits=bc + [mm_all], inc="v")
        out_ev = b.emit("v", lambda e, c=c: e.tensor_tensor(XL(c), XL(c), BB, ALU.add),
                        waits=[ev1], inc="v")
        apply_evs.append(out_ev)
    return out_ev, apply_evs


def emit_ffn0(b, ctx, ACC, XLN, w, ln_ev, acc_ev, lnch=None, pre_dep=None, wd_dep=None):
    if lnch is None:
        lnch = [ln_ev] * 16
    LNW = lambda c: lnch[c]
    PS = ctx["PS"]
    ROWS = ctx["ROWS"]
    SMALL = ctx["SMALL"]
    XL = lambda c: XLN[:, c * 1024:(c + 1) * 1024]
    A16 = lambda c: ACC[:, c * 1024:(c + 1) * 1024]
    SCR = cbf(ctx, 48, 59)
    HT = lambda ft: SCR[:, ft * 1024:(ft + 1) * 1024]
    CB = cbf(ctx, 59, 60)
    GB = cbf(ctx, 60, 61)
    SIL = cbf(ctx, 61, 62)
    USC = cbf(ctx, 62, 63)
    WG = [cbf(ctx, 72, 76), cbf(ctx, 76, 80)]
    WU = [cbf(ctx, 64, 68), cbf(ctx, 68, 72)]
    WDb = [ctx["ABH"][:, 80 * 1024:80 * 1024 + 1408],
           ctx["ABH"][:, 80 * 1024 + 1408:80 * 1024 + 2816]]
    C_T = ctx["C_T"][:, :]
    g_row = ROWS[0:1, 1024:2048]
    lg = SMALL[:, 0:8]
    negmax = SMALL[:, 8:9]
    lsum = SMALL[:, 9:10]
    rsum = SMALL[:, 10:11]
    m1 = SMALL[:, 11:12]
    m2c = SMALL[:, 12:13]
    t1 = SMALL[:, 16:24]
    keep = SMALL[:, 24:32]
    probs = SMALL[:, 32:40]

    b.dma("g", ctx["CONRT"][:, 0:128].rearrange("p (c e) -> p c e", e=8),
          w["Wr"].rearrange("(c p) e -> p c e", p=128), sem="dwr")
    wsg_ev = b.dma("g", ctx["CONRT"][:, 128:144], w["Wsg"], sem="dwr")
    wr_ev = wsg_ev
    Wr_sb = lambda c: ctx["CONRT"][:, c * 8:(c + 1) * 8]
    Wsg_sb = lambda c: ctx["CONRT"][:, 128 + c:129 + c]

    ct_last = None
    for tt in range(8):
        mme = None
        for c in range(16):
            mme = mm(b, PS, 0, lambda e, c=c, tt=tt: e.matmul(
                PS[0][:, 0:8], XL(c)[:, tt * 128:(tt + 1) * 128], Wr_sb(c),
                start=(c == 0), stop=(c == 15)), [LNW(c), wr_ev],
                start=(c == 0), inc="t" if c == 15 else None)
        ev = evac(b, "v", 0, lambda e: e.tensor_copy(lg, PS[0][:, 0:8]), [mme], "v")
        ev = b.emit("v", lambda e: e.reduce_max(negmax, lg, axis=AX.X, negate=True), waits=[ev], inc="v")
        ev = b.emit("s", lambda e: e.activation(probs, lg, AF.Exp, bias=negmax[:, 0:1]), waits=[ev], inc="s")
        ev = b.emit("v", lambda e: e.reduce_sum(lsum, probs, axis=AX.X), waits=[ev], inc="v")
        ev = b.emit("v", lambda e: e.reciprocal(rsum, lsum), waits=[ev], inc="v")
        ev = b.emit("v", lambda e: e.tensor_scalar_mul(probs, probs, rsum[:, 0:1]), waits=[ev], inc="v")
        ev = b.emit("v", lambda e: e.reduce_max(m1, probs, axis=AX.X), waits=[ev], inc="v")
        ev = b.emit("v", lambda e: e.tensor_scalar(t1, probs, m1[:, 0:1], None, ALU.is_ge), waits=[ev], inc="v")
        ev = b.emit("v", lambda e: e.scalar_tensor_tensor(t1, t1, -1.0e30, probs, ALU.mult, ALU.add),
                    waits=[ev], inc="v")
        ev = b.emit("v", lambda e: e.reduce_max(m2c, t1, axis=AX.X), waits=[ev], inc="v")
        ev = b.emit("v", lambda e: e.tensor_scalar(keep, probs, m2c[:, 0:1], None, ALU.is_ge), waits=[ev], inc="v")
        ev = b.emit("v", lambda e: e.tensor_mul(probs, probs, keep), waits=[ev], inc="v")
        tev = mm(b, PS, 1, lambda e: e.transpose(PS[1][0:8, 0:128], probs, ident(ctx)),
                 [ev], start=True, inc="t")
        ct_last = evac(b, "v", 1, lambda e, tt=tt: e.tensor_copy(
            C_T[:, tt * 128:(tt + 1) * 128], PS[1][0:8, 0:128]), [tev], "v")

    gmm = [None, None]
    for hh in range(2):
        for c in range(16):
            gmm[hh] = mm(b, PS, 2 + hh, lambda e, c=c, hh=hh: e.matmul(
                PS[2 + hh][0:1, 0:512], Wsg_sb(c), XL(c)[:, hh * 512:(hh + 1) * 512],
                start=(c == 0), stop=(c == 15)), [LNW(c), wsg_ev], start=(c == 0),
                inc="t" if c == 15 else None)
    sg0 = evac(b, "s", 2, lambda e: e.activation(g_row[:, 0:512], PS[2][0:1, :], AF.Sigmoid), [gmm[0]], "s")
    sg1 = evac(b, "s", 3, lambda e: e.activation(g_row[:, 512:1024], PS[3][0:1, :], AF.Sigmoid), [gmm[1]], "s")
    gb_last = None
    for hh in range(2):
        tev = mm(b, PS, 4 + hh, lambda e, hh=hh: e.matmul(
            PS[4 + hh][:, :], ones_row(ctx), g_row[:, hh * 512:(hh + 1) * 512],
            start=True, stop=True), [sg0, sg1], start=True, inc="t")
        gb_last = evac(b, "s", 4 + hh, lambda e, hh=hh: e.activation(
            GB[:, hh * 512:(hh + 1) * 512], PS[4 + hh][:, :], AF.Copy, scale=0.5), [tev], "s")

    def wgu_src(kind, ei, fg):
        wdt = 256 if fg < 5 else 128
        if ei < 8:
            t_ = w["Wg"] if kind == "g" else w["Wu"]
            return t_[ei].rearrange("(c p) f -> p c f", p=128)[:, :, fg * 256:fg * 256 + wdt]
        t_ = w["Sg"] if kind == "g" else w["Su"]
        col0 = (ei - 8) * 1408 + fg * 256
        return t_.rearrange("(c p) f -> p c f", p=128)[:, :, col0:col0 + wdt]

    def wd_src(ei, ot):
        if ei < 8:
            return w["Wd"][ei].rearrange("(c p) o -> p c o", p=128)[:, :, ot * 128:(ot + 1) * 128]
        r0 = (ei - 8) * 1408
        return w["Sd"][r0:r0 + 1408].rearrange("(c p) o -> p c o", p=128)[:, :, ot * 128:(ot + 1) * 128]

    WG3 = [t.rearrange("p (c f) -> p c f", f=256) for t in WG]
    WU3 = [t.rearrange("p (c f) -> p c f", f=256) for t in WU]
    WD3 = [t.rearrange("p (c f) -> p c f", f=128) for t in WDb]

    NEXP = 12
    gu_seq = [(ei, fg) for ei in range(NEXP) for fg in range(6)]
    wd_seq = [(ei, ot) for ei in range(NEXP) for ot in range(16)]
    wg_ld = {}
    wd_ld = {}
    wg_war = {0: [pre_dep] if pre_dep else [], 1: [wd_dep or pre_dep] if (wd_dep or pre_dep) else []}
    wu_war = {0: [ln_ev, pre_dep], 1: [ln_ev, pre_dep]}
    wd_war = {0: [wd_dep] if wd_dep else [], 1: [wd_dep] if wd_dep else []}

    def prefetch_gu(i):
        if i >= len(gu_seq):
            return
        ei, fg = gu_seq[i]
        slot = i % 2
        wdt = 256 if fg < 5 else 128
        wg_ld[(ei, fg)] = (
            b.dma("g", WG[slot][:, 0:16 * wdt].rearrange("p (c f) -> p c f", f=wdt),
                  wgu_src("g", ei, fg), waits=wg_war[slot], sem=f"dwg{slot}"),
            b.dma("g", WU[slot][:, 0:16 * wdt].rearrange("p (c f) -> p c f", f=wdt),
                  wgu_src("u", ei, fg), waits=wu_war[slot], sem=f"dwu{slot}"))

    def prefetch_wd(i):
        if i >= len(wd_seq):
            return
        ei, ot = wd_seq[i]
        slot = i % 2
        wd_ld[(ei, ot)] = b.dma("g", WD3[slot], wd_src(ei, ot), waits=wd_war[slot], sem=f"dwd{slot}")

    prefetch_gu(0)
    prefetch_gu(1)
    prefetch_wd(0)
    prefetch_wd(1)

    h_wr = [None] * 11
    h_rd = [None] * 11
    sil_rd = None
    usc_rd = None
    acc_upd = [acc_ev]
    gi = 0
    di = 0
    for ei in range(NEXP):
        if ei < 8:
            cb_last = None
            for hh in range(2):
                tev = mm(b, PS, 6 + hh, lambda e, ei=ei, hh=hh: e.matmul(
                    PS[6 + hh][:, :], sel8(ctx, ei), C_T[:, hh * 512:(hh + 1) * 512],
                    start=True, stop=True), [ct_last], start=True, inc="t")
                cb_last = evac(b, "s", 6 + hh, lambda e, hh=hh: e.activation(
                    CB[:, hh * 512:(hh + 1) * 512], PS[6 + hh][:, :], AF.Copy, scale=0.5),
                    [tev] + ([usc_rd] if usc_rd else []), "s")
            cb_cur, cb_tile = cb_last, CB
        else:
            cb_cur, cb_tile = gb_last, GB

        for fg in range(6):
            ldg, ldu = wg_ld[gu_seq[gi]]
            slot = gi % 2
            wdt = 256 if fg < 5 else 128
            ntile = 2 if fg < 5 else 1
            last_gm = None
            last_um = None
            for sub in range(ntile):
                ft = fg * 2 + sub
                gm = [None, None]
                um = [None, None]
                for c in range(16):
                    for hh in range(2):
                        gm[hh] = mm(b, PS, hh, lambda e, c=c, hh=hh, slot=slot, wdt=wdt, sub=sub: e.matmul(
                            PS[hh][:, :], WG[slot][:, c * wdt + sub * 128:c * wdt + sub * 128 + 128],
                            XL(c)[:, hh * 512:(hh + 1) * 512],
                            start=(c == 0), stop=(c == 15)), [ldg, LNW(c)], start=(c == 0),
                            inc="t" if c == 15 else None)
                for c in range(16):
                    for hh in range(2):
                        um[hh] = mm(b, PS, 2 + hh, lambda e, c=c, hh=hh, slot=slot, wdt=wdt, sub=sub: e.matmul(
                            PS[2 + hh][:, :], WU[slot][:, c * wdt + sub * 128:c * wdt + sub * 128 + 128],
                            XL(c)[:, hh * 512:(hh + 1) * 512],
                            start=(c == 0), stop=(c == 15)), [ldu, LNW(c)], start=(c == 0),
                            inc="t" if c == 15 else None)
                last_gm, last_um = gm, um
                sv = []
                uv = []
                for hh in range(2):
                    sv.append(evac(b, "s", hh, lambda e, hh=hh: e.activation(
                        SIL[:, hh * 512:(hh + 1) * 512], PS[hh][:, :], AF.Silu),
                        [gm[hh]] + ([sil_rd] if sil_rd else []), "s"))
                    uv.append(evac(b, "v", 2 + hh, lambda e, hh=hh, cb_tile=cb_tile: e.tensor_tensor(
                        USC[:, hh * 512:(hh + 1) * 512], PS[2 + hh][:, :],
                        cb_tile[:, hh * 512:(hh + 1) * 512], ALU.mult),
                        [um[hh], cb_cur] + ([usc_rd] if usc_rd else []), "v"))
                hv = b.emit("v", lambda e, ft=ft: e.tensor_tensor(HT(ft), SIL, USC, ALU.mult),
                            waits=sv + uv + ([h_rd[ft]] if h_rd[ft] else []), inc="v")
                h_wr[ft] = hv
                sil_rd = hv
                usc_rd = hv
            wg_war[slot] = [last_gm[1]]
            wu_war[slot] = [last_um[1]]
            gi += 1
            prefetch_gu(gi + 1)

        for ot in range(16):
            ldd = wd_ld[wd_seq[di]]
            slot = di % 2
            dm = [None, None]
            for fc in range(11):
                for hh in range(2):
                    dm[hh] = mm(b, PS, 4 + hh, lambda e, fc=fc, hh=hh, slot=slot: e.matmul(
                        PS[4 + hh][:, :], WD3[slot][:, fc, :], HT(fc)[:, hh * 512:(hh + 1) * 512],
                        start=(fc == 0), stop=(fc == 10)), [ldd, h_wr[fc]], start=(fc == 0),
                        inc="t" if fc == 10 else None)
            for hh in range(2):
                e_acc = evac(b, "v", 4 + hh, lambda e, ot=ot, hh=hh: e.tensor_tensor(
                    A16(ot)[:, hh * 512:(hh + 1) * 512], A16(ot)[:, hh * 512:(hh + 1) * 512],
                    PS[4 + hh][:, :], ALU.add), [dm[hh]] + acc_upd, "v")
            acc_upd = [e_acc]
            wd_war[slot] = [e_acc]
            di += 1
            prefetch_wd(di + 1)
        for ft in range(11):
            h_rd[ft] = dm[1]
    return acc_upd[0]


def decl_ffn0_weights(nc, pre=""):
    w = {}
    w["Wr"] = nc.dram_tensor(pre + "Wr", [2048, 8], F32, kind="ExternalInput").ap()
    w["Wsg"] = nc.dram_tensor(pre + "Wsg", [128, 16], F32, kind="ExternalInput").ap()
    w["Wg"] = nc.dram_tensor(pre + "Wg", [8, 2048, 1408], F32, kind="ExternalInput").ap()
    w["Wu"] = nc.dram_tensor(pre + "Wu", [8, 2048, 1408], F32, kind="ExternalInput").ap()
    w["Wd"] = nc.dram_tensor(pre + "Wd", [8, 1408, 2048], F32, kind="ExternalInput").ap()
    w["Sg"] = nc.dram_tensor(pre + "Sg", [2048, 5632], F32, kind="ExternalInput").ap()
    w["Su"] = nc.dram_tensor(pre + "Su", [2048, 5632], F32, kind="ExternalInput").ap()
    w["Sd"] = nc.dram_tensor(pre + "Sd", [5632, 2048], F32, kind="ExternalInput").ap()
    return w


def build_ffn0_program():
    nc = bass.Bass("TRN2", target_bir_lowering=False)
    b = B(nc)
    x_in = nc.dram_tensor("x_in", [2048, 1024], F32, kind="ExternalInput").ap()
    conf_in = nc.dram_tensor("conf", [128, 1312], F32, kind="ExternalInput").ap()
    conb_in = nc.dram_tensor("conb", [128, 2177], BF16, kind="ExternalInput").ap()
    w = decl_ffn0_weights(nc)
    x_out = nc.dram_tensor("x_out", [2048, 1024], F32, kind="ExternalOutput").ap()
    ln_out = nc.dram_tensor("ln_out", [2048, 1024], BF16, kind="ExternalOutput").ap()

    ctx = decl_common(nc, b)
    with nc.Block() as block:
        ACC = cf32(ctx, 16, 48)
        XLN = cbf(ctx, 0, 16)
        xb = cbf(ctx, 64, 68)
        c_ev = emit_load_consts(b, ctx, conf_in, conb_in)
        acc_ev = b.dma("y", ACC.rearrange("p (c t) -> p c t", t=1024),
                       x_in.rearrange("(c p) t -> p c t", p=128), sem="dx")
        ln_ev, lnch = emit_ln(b, ctx, ACC, XLN, xb, deps=[acc_ev] + c_ev)
        fin = emit_ffn0(b, ctx, ACC, XLN, w, ln_ev, acc_ev, lnch=lnch)
        ln2, lnch2 = emit_ln(b, ctx, ACC, XLN, xb, deps=[fin])
        st1 = b.dma("y", x_out.rearrange("(c p) t -> p c t", p=128),
                    ACC.rearrange("p (c t) -> p c t", t=1024), waits=[fin], sem="dst")
        st2 = b.dma("y", ln_out.rearrange("(c p) t -> p c t", p=128),
                    XLN.rearrange("p (c t) -> p c t", t=1024), waits=[ln2], sem="dst")
        b.emit("y", lambda e: e.nop(), waits=[st1, st2])
        b.finalize(block)
    return nc


def emit_attn(b, ctx, lnfull, contrib, wq_in, wo_in, bias, deps, wdeps=None, half_deps=None):
    PS = ctx["PS"]
    ROWS = ctx["ROWS"]
    SMALL = ctx["SMALL"]
    Q = [cbf(ctx, 0, 8), cbf(ctx, 8, 16)]
    Kb = [cbf(ctx, 16, 24), cbf(ctx, 24, 32)]
    Vb = [cbf(ctx, 32, 40), cbf(ctx, 40, 48)]
    WQK = cbf(ctx, 48, 56)
    WV = cbf(ctx, 56, 60)
    XST = [cbf(ctx, 60, 68), cbf(ctx, 68, 76)]
    PT = [cbf(ctx, 76, 77)[:, 0:512], cbf(ctx, 76, 77)[:, 512:1024]]
    CST = [cbf(ctx, 78, 79)[:, 0:512], cbf(ctx, 78, 79)[:, 512:1024]]
    LINV = cf32(ctx, 80, 82)
    NS = ctx["ABH"][:, 82 * 1024:82 * 1024 + 512]
    BV = ctx["ABH"][:, 82 * 1024 + 512:82 * 1024 + 768]
    WQK4 = WQK.rearrange("p (j c f) -> p j c f", c=16, f=128)
    WV3 = WV.rearrange("p (c f) -> p c f", f=256)
    XST3 = [t.rearrange("p (c f) -> p c f", f=512) for t in XST]
    lnf = [t.rearrange("(r c p) t -> r p c t", p=128, c=16) for t in lnfull]
    bqk_sb = SMALL[:, 48:54]
    bqs_sb = SMALL[:, 54:60]
    lrow = ROWS[0:1, 0:512]
    nrm = ROWS[0:1, 512:1024]
    qmax = SMALL[0:1, 40:42]
    kmax = SMALL[0:1, 42:44]
    tmp1 = SMALL[0:1, 44:45]
    negM = SMALL[:, 32:34]

    if wdeps is None:
        wdeps = deps
    for j in range(4):
        b.dma("g", WQK4[:, j, :, :],
              wq_in[:, j * 128:(j + 1) * 128].rearrange("(c p) o -> p c o", p=128),
              waits=wdeps, sem="dwq")
    wv_ev = b.dma("g", WV3, wq_in[:, 512:768].rearrange("(c p) o -> p c o", p=128),
                  waits=wdeps, sem="dwq")
    wq_ev = [wv_ev] * 4
    bias_ev = []
    if bias is not None:
        bqk_in, bv_in = bias
        b.dma("y", bqk_sb, bqk_in, waits=wdeps, sem="dbi")
        bve = b.dma("y", ROWS[0:1, 1024:1280], bv_in, waits=wdeps, sem="dbi")
        bs = bve
        t1 = mm(b, PS, 7, lambda e: e.matmul(PS[7][:, 0:256], ones_row(ctx), ROWS[0:1, 1024:1280],
                                             start=True, stop=True), [bve], start=True, inc="t")
        bvev = evac(b, "v", 7, lambda e: e.tensor_copy(BV, PS[7][:, 0:256]), [t1], "v")
        bias_ev = [bs, bvev]

    stage_war = [None, None]
    ev_v_done = []
    nrm_acc = {}
    ns_war = None
    last_red = None
    qkv_mm_last = None
    for r in range(8):
        for tb in range(2):
            g = r * 2 + tb
            buf = g % 2
            hdep = [half_deps[tb]] if half_deps else deps
            st_ev = b.dma("y", XST3[buf], lnf[tb][r, :, :, :],
                          waits=([stage_war[buf]] if stage_war[buf] else []) + hdep,
                          sem=f"dxs{buf}")
            col0 = g * 512
            last_mm = None
            qk_evs = []
            for j in range(4):
                mme = None
                for c in range(16):
                    mme = mm(b, PS, j, lambda e, j=j, c=c, buf=buf: e.matmul(
                        PS[j][:, :], WQK4[:, j, c, :], XST3[buf][:, c, :],
                        start=(c == 0), stop=(c == 15)), [st_ev, wq_ev[j]], start=(c == 0),
                        inc="t" if c == 15 else None)
                dstt = (Q[j] if j < 2 else Kb[j - 2])[:, col0:col0 + 512]
                if bias is not None:
                    eee = evac(b, "v", j, lambda e, dstt=dstt, j=j: e.tensor_scalar(
                        dstt, PS[j][:, :], bqk_sb[:, j:j + 1], QSC, ALU.add, ALU.mult),
                        [mme] + bias_ev, "v")
                else:
                    eee = evac(b, "s", j, lambda e, dstt=dstt, j=j: e.activation(
                        dstt, PS[j][:, :], AF.Copy, scale=QSC), [mme], "s")
                qk_evs.append((j, dstt, eee))
                last_mm = mme
            for tt in range(4):
                mme = None
                for c in range(16):
                    mme = mm(b, PS, 4, lambda e, tt=tt, c=c, buf=buf: e.matmul(
                        PS[4][:, 0:256], XST3[buf][:, c, tt * 128:(tt + 1) * 128], WV3[:, c, :],
                        start=(c == 0), stop=(c == 15)), [st_ev, wv_ev], start=(c == 0),
                        inc="t" if c == 15 else None)
                gtt = g * 4 + tt
                for h in range(2):
                    if bias is not None:
                        eee = evac(b, "v", 4, lambda e, h=h, gtt=gtt: e.tensor_tensor(
                            Vb[h][:, gtt * 128:(gtt + 1) * 128], PS[4][:, h * 128:(h + 1) * 128],
                            BV[:, h * 128:(h + 1) * 128], ALU.add), [mme] + bias_ev, "v")
                    else:
                        eee = evac(b, "v", 4, lambda e, h=h, gtt=gtt: e.tensor_copy(
                            Vb[h][:, gtt * 128:(gtt + 1) * 128], PS[4][:, h * 128:(h + 1) * 128]),
                            [mme], "v")
                    ev_v_done.append(eee)
                last_mm = mme
            stage_war[buf] = last_mm
            qkv_mm_last = last_mm
            for (j, dstt, eee) in qk_evs:
                key = j
                dst = qmax[:, j:j + 1] if j < 2 else kmax[:, j - 2:j - 1]
                sq_ev = b.emit("s", lambda e, dstt=dstt: e.activation(
                    NS, dstt, AF.Square),
                    waits=[eee, ns_war], inc="s")
                t_ev = mm(b, PS, 6, lambda e: e.matmul(
                    PS[6][0:1, 0:512], ones_bf(ctx), NS, start=True, stop=True),
                    [sq_ev], start=True, inc="t")
                ns_war = t_ev
                r_ev = b.emit("v", lambda e: e.reduce_max(tmp1, PS[6][0:1, 0:512], axis=AX.X),
                              waits=[t_ev, last_red], inc="v")
                b.bank[6] = r_ev
                if key not in nrm_acc:
                    a_ev = b.emit("v", lambda e, dst=dst: e.tensor_copy(dst, tmp1),
                                  waits=[r_ev], inc="v")
                else:
                    a_ev = b.emit("v", lambda e, dst=dst: e.tensor_max(dst, dst, tmp1),
                                  waits=[r_ev, nrm_acc[key]], inc="v")
                nrm_acc[key] = a_ev
                last_red = a_ev
    qkv_done = ev_v_done[-1]
    nrm_ev = [nrm_acc[j] for j in range(4)]
    negm_ev = []
    for h in range(2):
        s_ev = b.emit("v", lambda e, h=h: e.tensor_add(tmp1, qmax[:, h:h + 1], kmax[:, h:h + 1]),
                      waits=nrm_ev, inc="v")
        s_ev = b.emit("v", lambda e: e.tensor_scalar_mul(tmp1, tmp1, -0.5), waits=[s_ev], inc="v")
        t_ev = mm(b, PS, 6, lambda e: e.matmul(PS[6][:, 0:1], ones_row(ctx), tmp1,
                                               start=True, stop=True), [s_ev], start=True, inc="t")
        negm_ev.append(evac(b, "v", 6, lambda e, h=h: e.tensor_copy(negM[:, h:h + 1], PS[6][:, 0:1]),
                            [t_ev], "v"))

    PT4 = [PT[0], PT[1],
           cbf(ctx, 78, 79)[:, 0:512], cbf(ctx, 78, 79)[:, 512:1024]]
    LSC = [NS, cbf(ctx, 79, 80)[:, 0:512]]
    lrow2 = [lrow, ROWS[0:1, 512:1024]]
    LINV2 = [LINV[:, 0:512], LINV[:, 512:1024]]
    pt_war4 = [None, None, None, None]
    lsc_war = [None, None]
    attn_done = []
    for h in range(2):
        for qi in range(16):
            nkt = 4 * (qi + 1)
            pvb = 4 if qi % 2 == 0 else 7
            pv_mm = None
            ex_evs = [None] * nkt
            for step in range(nkt + 2):
                ktg = step
                if ktg < nkt:
                    sl = ktg % 4
                    sc_mm = mm(b, PS, sl, lambda e, h=h, qi=qi, ktg=ktg, sl=sl: e.matmul(
                        PS[sl][:, :], Kb[h][:, ktg * 128:(ktg + 1) * 128],
                        Q[h][:, qi * 512:(qi + 1) * 512], start=True, stop=True,
                        skip_group_check=True),
                        [qkv_done] + negm_ev, start=True, inc="t")
                    ex_ev = evac(b, "s", sl, lambda e, sl=sl, h=h: e.activation(
                        PT4[sl], PS[sl][:, :], AF.Exp, bias=negM[:, h:h + 1]),
                        [sc_mm] + ([pt_war4[sl]] if pt_war4[sl] else []), "s")
                    if ktg // 4 == qi:
                        ex_ev = b.emit("v", lambda e, sl=sl, kt=ktg % 4: e.tensor_tensor(
                            PT4[sl], PT4[sl], dmask(ctx, kt), ALU.mult), waits=[ex_ev], inc="v")
                    ex_evs[ktg] = ex_ev
                kt2 = step - 2
                if kt2 >= 0:
                    sl2 = kt2 % 4
                    if kt2 % 2 == 1:
                        lb2 = (kt2 // 2) % 2
                        se_ = b.emit("v", lambda e, sl2=sl2, lb2=lb2: e.tensor_add(
                            LSC[lb2], PT4[sl2 - 1], PT4[sl2]),
                            waits=[ex_evs[kt2 - 1], ex_evs[kt2], lsc_war[lb2]], inc="v")
                        mm(b, PS, 5, lambda e, lb2=lb2, kt2=kt2, nkt=nkt: e.matmul(
                            PS[5][0:1, :], ones_bf(ctx), LSC[lb2],
                            start=(kt2 == 1), stop=(kt2 == nkt - 1), skip_group_check=True),
                            [se_], start=(kt2 == 1), inc=None)
                    pv_mm = mm(b, PS, pvb, lambda e, h=h, kt2=kt2, sl2=sl2, nkt=nkt, pvb=pvb: e.matmul(
                        PS[pvb][:, :], Vb[h][:, kt2 * 128:(kt2 + 1) * 128], PT4[sl2],
                        start=(kt2 == 0), stop=(kt2 == nkt - 1), skip_group_check=True),
                        [ex_evs[kt2]], start=(kt2 == 0), inc="t" if kt2 == nkt - 1 else None)
                    pt_war4[sl2] = pv_mm
                    if kt2 % 2 == 1:
                        lsc_war[(kt2 // 2) % 2] = pv_mm
            qp = qi % 2
            le = evac(b, "v", 5, lambda e, qp=qp: e.tensor_copy(lrow2[qp], PS[5][0:1, :]),
                      [pv_mm], "v")
            le = b.emit("v", lambda e, qp=qp: e.reciprocal(lrow2[qp], lrow2[qp]),
                        waits=[le], inc="v")
            t_ev = mm(b, PS, 6, lambda e, qp=qp: e.matmul(PS[6][:, :], ones_row(ctx), lrow2[qp],
                                                          start=True, stop=True),
                      [le], start=True, inc="t")
            lb = evac(b, "v", 6, lambda e, qp=qp: e.tensor_copy(LINV2[qp], PS[6][:, :]),
                      [t_ev], "v")
            ae = evac(b, "v", pvb, lambda e, h=h, qi=qi, qp=qp, pvb=pvb: e.tensor_tensor(
                Q[h][:, qi * 512:(qi + 1) * 512], PS[pvb][:, :], LINV2[qp], ALU.mult),
                [lb, pv_mm], "v")
            attn_done.append(ae)
    at_all = attn_done[-1]

    WO = cbf(ctx, 48, 52)
    WO3 = WO.rearrange("p (c o) -> p c o", o=2048)
    wo_ev = b.dma("g", WO3, wo_in.rearrange("(c p) o -> p c o", p=128),
                  waits=[qkv_mm_last], sem="dwq")
    cst_war = [None, None]
    half_stores = []
    mme = None
    for hw in range(2):
        coloff = hw * 512
        for rr in range(8):
            tb = rr * 2 + hw
            for ot in range(16):
                sl = ot % 2
                mme = None
                for hch in range(2):
                    mme = mm(b, PS, sl, lambda e, tb=tb, ot=ot, hch=hch, sl=sl: e.matmul(
                        PS[sl][:, :], WO3[:, hch, ot * 128:(ot + 1) * 128],
                        Q[hch][:, tb * 512:(tb + 1) * 512],
                        start=(hch == 0), stop=(hch == 1)), [wo_ev, at_all], start=(hch == 0),
                        inc="t" if hch == 1 else None)
                ce = evac(b, "v", sl, lambda e, sl=sl: e.tensor_copy(CST[sl], PS[sl][:, :]),
                          [mme] + ([cst_war[sl]] if cst_war[sl] else []), "v")
                se = b.dma("y", contrib[hw][rr * 2048 + ot * 128: rr * 2048 + (ot + 1) * 128, :],
                           CST[sl], waits=[ce], sem=f"dcs{sl}")
                cst_war[sl] = se
        half_stores.append([cst_war[0], cst_war[1]])
    return {"half_stores": half_stores, "stores": [cst_war[0], cst_war[1]], "at_all": at_all,
            "outproj_mm": mme, "qkv_mm_last": qkv_mm_last, "qkv_done": qkv_done}


def build_att_program(with_bias):
    nc = bass.Bass("TRN2", target_bir_lowering=False)
    b = B(nc)
    lnfull = nc.dram_tensor("lnfull", [16384, 1024], BF16, kind="ExternalInput").ap()
    wq_in = nc.dram_tensor("wq", [2048, 768], F32, kind="ExternalInput").ap()
    wo_in = nc.dram_tensor("wo", [256, 2048], F32, kind="ExternalInput").ap()
    conf_in = nc.dram_tensor("conf", [128, 1312], F32, kind="ExternalInput").ap()
    conb_in = nc.dram_tensor("conb", [128, 2177], BF16, kind="ExternalInput").ap()
    bias = None
    if with_bias:
        bqk_in = nc.dram_tensor("bqk", [128, 6], F32, kind="ExternalInput").ap()
        bv_in = nc.dram_tensor("bv", [1, 256], F32, kind="ExternalInput").ap()
        bias = (bqk_in, bv_in)
    contrib = nc.dram_tensor("contrib", [16384, 1024], BF16, kind="ExternalOutput").ap()
    ctx = decl_common(nc, b)
    with nc.Block() as block:
        c_ev = emit_load_consts(b, ctx, conf_in, conb_in)
        last = emit_attn(b, ctx, [lnfull[:, 0:512], lnfull[:, 512:1024]],
                         [contrib[:, 0:512], contrib[:, 512:1024]],
                         wq_in, wo_in, bias, deps=c_ev)
        b.emit("y", lambda e: e.nop(), waits=last["stores"])
        b.finalize(block)
    return nc


def emit_ffn1(b, ctx, XLN, w, x_in, x_out, ln_ev, deps, lnch=None):
    if lnch is None:
        lnch = [ln_ev] * 16
    LNW = lambda c: lnch[c]
    """w: dict W1T [2048,8192], W3T [2048,8192], W2T [8192,2048] dram f32.
    x_in/x_out: dram [2048, 1024] f32. Returns last x-store event."""
    PS = ctx["PS"]
    XL = lambda c: XLN[:, c * 1024:(c + 1) * 1024]
    HID = cbf(ctx, 16, 48)
    HT1 = lambda fc: HID[:, fc * 512:(fc + 1) * 512]
    W1b = [cbf(ctx, 48, 50), cbf(ctx, 50, 52)]
    W3b = [cbf(ctx, 52, 54), cbf(ctx, 54, 56)]
    W2b = [cbf(ctx, 56, 60), cbf(ctx, 60, 64), cbf(ctx, 68, 72), cbf(ctx, 72, 76)]
    SILB = ctx["ABH"][:, 82 * 1024:82 * 1024 + 512]
    XR = [cf32(ctx, 64, 66)[:, 0:512], cf32(ctx, 64, 66)[:, 512:1024]]
    W1b3 = [t.rearrange("p (c f) -> p c f", f=128) for t in W1b]
    W3b3 = [t.rearrange("p (c f) -> p c f", f=128) for t in W3b]
    W2b3 = [t.rearrange("p (c f) -> p c f", f=128) for t in W2b]
    w1s = w["W1T"].rearrange("(c p) f -> p c f", p=128)
    w3s = w["W3T"].rearrange("(c p) f -> p c f", p=128)
    w2s = w["W2T"].rearrange("(c p) o -> p c o", p=128)
    xi3 = x_in.rearrange("(c p) t -> p c t", p=128)
    xo3 = x_out.rearrange("(c p) t -> p c t", p=128)
    HSC = float(1.0 / np.sqrt(8192.0))

    st_last = None
    w1_war = {0: [], 1: []}
    w3_war = {0: [], 1: []}
    w2_war = {0: [], 1: [], 2: [], 3: []}
    ld1 = {}
    ld3 = {}
    ld2 = {}
    h_ev = [None] * 64
    sil_war = None
    xr_war = [None, None]
    for half in range(2):
        hc = slice(half * 512, (half + 1) * 512)

        def pf(j, half=half):
            if j >= 64:
                return
            sl = j % 2
            ld1[(half, j)] = b.dma("g", W1b3[sl], w1s[:, :, j * 128:(j + 1) * 128],
                                   waits=w1_war[sl] + deps, sem=f"dw1{sl}")
            ld3[(half, j)] = b.dma("g", W3b3[sl], w3s[:, :, j * 128:(j + 1) * 128],
                                   waits=w3_war[sl] + deps, sem=f"dw3{sl}")

        def h128_and_bfly(j):
            pgh = 2 + (j % 2) * 4
            te = mm(b, PS, pgh, lambda e, j=j, pgh=pgh: e.matmul(
                PS[pgh][:, :], h128c(ctx), HT1(j), start=True, stop=True,
                skip_group_check=True), [h_ev[j]], start=True, inc="t")
            h_ev[j] = evac(b, "s", pgh, lambda e, j=j, pgh=pgh: e.activation(
                HT1(j), PS[pgh][:, :], AF.Copy, scale=HSC), [te], "s")
            for st in range(6):
                bs = 1 << (st + 1)
                d = 1 << st
                if (j + 1) % bs == 0:
                    base = j + 1 - bs
                    for off in range(d):
                        a_i = base + off
                        b_i = base + off + d
                        e1 = b.emit("v", lambda e, a_i=a_i, b_i=b_i: e.tensor_add(
                            HT1(a_i), HT1(a_i), HT1(b_i)),
                            waits=[h_ev[a_i], h_ev[b_i]], inc="v")
                        e2 = b.emit("v", lambda e, a_i=a_i, b_i=b_i: e.scalar_tensor_tensor(
                            HT1(b_i), HT1(b_i), -2.0, HT1(a_i), ALU.mult, ALU.add),
                            waits=[e1], inc="v")
                        h_ev[a_i] = e2
                        h_ev[b_i] = e2
        pf(0)
        pf(1)
        pending = None
        for j in range(64):
            sl = j % 2
            pg = (j % 2) * 4
            gm = None
            um = None
            for c in range(16):
                gm = mm(b, PS, pg, lambda e, c=c, sl=sl, pg=pg, hc=hc: e.matmul(
                    PS[pg][:, :], W1b3[sl][:, c, :], XL(c)[:, hc],
                    start=(c == 0), stop=(c == 15)), [ld1[(half, j)], LNW(c)], start=(c == 0),
                    inc="t" if c == 15 else None)
            for c in range(16):
                um = mm(b, PS, pg + 1, lambda e, c=c, sl=sl, pg=pg, hc=hc: e.matmul(
                    PS[pg + 1][:, :], W3b3[sl][:, c, :], XL(c)[:, hc],
                    start=(c == 0), stop=(c == 15)), [ld3[(half, j)], LNW(c)], start=(c == 0),
                    inc="t" if c == 15 else None)
            w1_war[sl] = [gm]
            w3_war[sl] = [um]
            sv = evac(b, "s", pg, lambda e, pg=pg: e.activation(SILB, PS[pg][:, :], AF.Silu),
                      [gm] + ([sil_war] if sil_war else []), "s")
            hv = evac(b, "v", pg + 1, lambda e, j=j, pg=pg: e.tensor_tensor(
                HT1(j), PS[pg + 1][:, :], SILB, ALU.mult),
                [um, sv] + ([h_ev[j]] if h_ev[j] else []), "v")
            sil_war = hv
            h_ev[j] = hv
            pf(j + 2)
            if pending is not None:
                h128_and_bfly(pending)
            pending = j
        h128_and_bfly(pending)
        def pf2(i, half=half):
            if i >= 32:
                return
            ot, hw = i // 2, i % 2
            sl = i % 4
            ld2[(half, i)] = b.dma("g", W2b3[sl],
                                   w2s[:, hw * 32:(hw + 1) * 32, ot * 128:(ot + 1) * 128],
                                   waits=w2_war[sl] + deps, sem=f"dw2{sl}")
        for i in range(4):
            pf2(i)
        for ot in range(16):
            xsl = ot % 2
            pg3 = 3 + (ot % 2) * 4
            xe = b.dma("y", XR[xsl], xi3[:, ot, hc],
                       waits=([xr_war[xsl]] if xr_war[xsl] else [ln_ev]) + deps,
                       sem=f"dxr{xsl}")
            dm = None
            for i2 in (2 * ot, 2 * ot + 1):
                sl = i2 % 4
                hw = i2 % 2
                for k in range(32):
                    fc = hw * 32 + k
                    dm = mm(b, PS, pg3, lambda e, k=k, sl=sl, fc=fc, pg3=pg3: e.matmul(
                        PS[pg3][:, :], W2b3[sl][:, k, :], HT1(fc),
                        start=(fc == 0), stop=(fc == 63)), [ld2[(half, i2)], h_ev[fc]],
                        start=(fc == 0), inc="t" if k == 31 else None)
                w2_war[sl] = [dm]
                pf2(i2 + 4)
            ae = evac(b, "v", pg3, lambda e, xsl=xsl, pg3=pg3: e.scalar_tensor_tensor(
                XR[xsl], PS[pg3][:, :], 0.5, XR[xsl], ALU.mult, ALU.add), [dm, xe], "v")
            st_last = b.dma("y", xo3[:, ot, hc], XR[xsl], waits=[ae], sem=f"dxw{xsl}")
            xr_war[xsl] = st_last
    return [xr_war[0], xr_war[1]]


def decl_ffn1_weights(nc, pre=""):
    w = {}
    w["W1T"] = nc.dram_tensor(pre + "W1T", [2048, 8192], F32, kind="ExternalInput").ap()
    w["W3T"] = nc.dram_tensor(pre + "W3T", [2048, 8192], F32, kind="ExternalInput").ap()
    w["W2T"] = nc.dram_tensor(pre + "W2T", [8192, 2048], F32, kind="ExternalInput").ap()
    return w


def build_ffn1_program():
    nc = bass.Bass("TRN2", target_bir_lowering=False)
    b = B(nc)
    x_in = nc.dram_tensor("x_in", [2048, 1024], F32, kind="ExternalInput").ap()
    conf_in = nc.dram_tensor("conf", [128, 1312], F32, kind="ExternalInput").ap()
    conb_in = nc.dram_tensor("conb", [128, 2177], BF16, kind="ExternalInput").ap()
    w = decl_ffn1_weights(nc)
    x_out = nc.dram_tensor("x_out", [2048, 1024], F32, kind="ExternalOutput").ap()
    ctx = decl_common(nc, b)
    with nc.Block() as block:
        ACC = cf32(ctx, 16, 48)
        XLN = cbf(ctx, 0, 16)
        xb = cbf(ctx, 64, 68)
        c_ev = emit_load_consts(b, ctx, conf_in, conb_in)
        acc_ev = b.dma("y", ACC.rearrange("p (c t) -> p c t", t=1024),
                       x_in.rearrange("(c p) t -> p c t", p=128), sem="dx")
        ln_ev, lnch = emit_ln(b, ctx, ACC, XLN, xb, deps=[acc_ev] + c_ev)
        st = emit_ffn1(b, ctx, XLN, w, x_in, x_out, ln_ev, deps=[ln_ev], lnch=lnch)
        b.emit("y", lambda e: e.nop(), waits=st)
        b.finalize(block)
    return nc


def emit_resid_parts(b, ctx, ACC, parts, bias_pc, deps):
    PB3 = cbf(ctx, 48, 64).rearrange("p (c t) -> p c t", t=1024)
    A16 = lambda c: ACC[:, c * 1024:(c + 1) * 1024]
    bias_ev = []
    if bias_pc is not None:
        be = b.dma("y", ctx["SMALL"][:, 16:32], bias_pc, waits=deps, sem="dbp")
        bias_ev = [be]
    last = None
    for r in range(8):
        ld = b.dma("y", PB3, parts[r].rearrange("(c p) t -> p c t", p=128),
                   waits=deps + ([last] if last else []), sem="dpr")
        for c in range(16):
            last = b.emit("v", lambda e, c=c: e.tensor_tensor(
                A16(c), A16(c), PB3[:, c, :], ALU.add),
                waits=[ld] + ([last] if last else []) + deps, inc="v")
    if bias_pc is not None:
        for c in range(16):
            last = b.emit("v", lambda e, c=c: e.tensor_scalar_add(
                A16(c), A16(c), ctx["SMALL"][:, 16 + c:17 + c]),
                waits=bias_ev + [last], inc="v")
    return last


def build_mid_program(final):
    nc = bass.Bass("TRN2", target_bir_lowering=False)
    b = B(nc)
    x_in = nc.dram_tensor("x_in", [2048, 1024], F32, kind="ExternalInput").ap()
    parts = nc.dram_tensor("parts", [8, 2048, 1024], BF16, kind="ExternalInput").ap()
    conf_in = nc.dram_tensor("conf", [128, 1312], F32, kind="ExternalInput").ap()
    conb_in = nc.dram_tensor("conb", [128, 2177], BF16, kind="ExternalInput").ap()
    w1 = decl_ffn1_weights(nc)
    x_out = nc.dram_tensor("x_out", [2048, 1024], F32, kind="ExternalOutput").ap()
    xm = nc.dram_tensor("xm", [2048, 1024], F32, kind="Internal").ap()
    if final:
        bias_pc = nc.dram_tensor("bias_pc", [128, 16], F32, kind="ExternalInput").ap()
        w0 = None
        ln_out = None
    else:
        bias_pc = None
        w0 = decl_ffn0_weights(nc)
        ln_out = nc.dram_tensor("ln_out", [2048, 1024], BF16, kind="ExternalOutput").ap()

    ctx = decl_common(nc, b)
    with nc.Block() as block:
        ACC = cf32(ctx, 16, 48)
        XLN = cbf(ctx, 0, 16)
        xb = cbf(ctx, 64, 68)
        c_ev = emit_load_consts(b, ctx, conf_in, conb_in)
        acc_ev = b.dma("y", ACC.rearrange("p (c t) -> p c t", t=1024),
                       x_in.rearrange("(c p) t -> p c t", p=128), sem="dx")
        rp = emit_resid_parts(b, ctx, ACC, parts, bias_pc, deps=[acc_ev] + c_ev)
        ln1, lnch1 = emit_ln(b, ctx, ACC, XLN, xb, deps=[rp])
        if not final:
            fin0 = emit_ffn0(b, ctx, ACC, XLN, w0, ln1, rp, lnch=lnch1)
            ln2, lnch2b = emit_ln(b, ctx, ACC, XLN, xb, deps=[fin0])
        else:
            fin0 = rp
            ln2, lnch2b = ln1, lnch1
        stm = b.dma("y", xm.rearrange("(c p) t -> p c t", p=128),
                    ACC.rearrange("p (c t) -> p c t", t=1024), waits=[fin0, ln2], sem="dst")
        stf = emit_ffn1(b, ctx, XLN, w1, xm, x_out, ln2, deps=[stm, ln2], lnch=lnch2b)
        if not final:
            acc2 = b.dma("y", ACC.rearrange("p (c t) -> p c t", t=1024),
                         x_out.rearrange("(c p) t -> p c t", p=128), waits=stf, sem="dx")
            ln3, _l3 = emit_ln(b, ctx, ACC, XLN, xb, deps=[acc2])
            st2 = b.dma("y", ln_out.rearrange("(c p) t -> p c t", p=128),
                        XLN.rearrange("p (c t) -> p c t", t=1024), waits=[ln3], sem="dst")
            b.emit("y", lambda e: e.nop(), waits=[st2] + stf)
        else:
            b.emit("y", lambda e: e.nop(), waits=stf)
        b.finalize(block)
    return nc


_PROGRAMS = {}


def _get_program(name):
    if name not in _PROGRAMS:
        if name == "ffn0a":
            _PROGRAMS[name] = build_ffn0_program()
        elif name == "att0":
            _PROGRAMS[name] = build_att_program(with_bias=False)
        elif name == "att1":
            _PROGRAMS[name] = build_att_program(with_bias=True)
        elif name == "mid":
            _PROGRAMS[name] = build_mid_program(final=False)
        elif name == "end":
            _PROGRAMS[name] = build_mid_program(final=True)
    return _PROGRAMS[name]


def kernel_unfused(L0_attn_hidden_states, L0_attn_w_qkv, L0_attn_w_o,
           L0_ffn_Wr, L0_ffn_Wg, L0_ffn_Wu, L0_ffn_Wd,
           L0_ffn_Sg, L0_ffn_Su, L0_ffn_Sd, L0_ffn_Wsg,
           L1_attn_w_attn, L1_attn_b_attn, L1_attn_w_proj, L1_attn_b_proj,
           L1_ffn_W1, L1_ffn_W2, L1_ffn_W3):
    x0 = np.ascontiguousarray(np.asarray(L0_attn_hidden_states, dtype=np.float32)[0])
    conf, conb = make_consts()
    CC = list(range(NCORES))

    w0 = {
        "Wr": np.ascontiguousarray(np.asarray(L0_ffn_Wr, np.float32)),
        "Wsg": np.ascontiguousarray(np.asarray(L0_ffn_Wsg, np.float32).reshape(16, 128).T),
        "Wg": np.ascontiguousarray(np.asarray(L0_ffn_Wg, np.float32)),
        "Wu": np.ascontiguousarray(np.asarray(L0_ffn_Wu, np.float32)),
        "Wd": np.ascontiguousarray(np.asarray(L0_ffn_Wd, np.float32)),
        "Sg": np.ascontiguousarray(np.asarray(L0_ffn_Sg, np.float32)),
        "Su": np.ascontiguousarray(np.asarray(L0_ffn_Su, np.float32)),
        "Sd": np.ascontiguousarray(np.asarray(L0_ffn_Sd, np.float32)),
    }
    w1 = {
        "W1T": np.ascontiguousarray(np.asarray(L1_ffn_W1, np.float32).T),
        "W3T": np.ascontiguousarray(np.asarray(L1_ffn_W3, np.float32).T),
        "W2T": np.ascontiguousarray(np.asarray(L1_ffn_W2, np.float32).T),
    }
    wqkvT0 = np.ascontiguousarray(np.asarray(L0_attn_w_qkv, np.float32).T)
    woT0 = np.ascontiguousarray(np.asarray(L0_attn_w_o, np.float32).T)
    wqkvT1 = np.ascontiguousarray(np.asarray(L1_attn_w_attn, np.float32))
    woT1 = np.ascontiguousarray(np.asarray(L1_attn_w_proj, np.float32))
    b_attn = np.asarray(L1_attn_b_attn, np.float32)
    b_proj = np.asarray(L1_attn_b_proj, np.float32)
    bias_pc = np.ascontiguousarray(b_proj.reshape(16, 128).T)

    def att_slices(wqkvT, woT):
        wq, wo = [], []
        for c in CC:
            h0 = 2 * c
            wq.append(np.ascontiguousarray(np.concatenate(
                [wqkvT[:, h0 * 128:(h0 + 2) * 128],
                 wqkvT[:, 2048 + h0 * 128:2048 + (h0 + 2) * 128],
                 wqkvT[:, 4096 + h0 * 128:4096 + (h0 + 2) * 128]], axis=1)))
            wo.append(np.ascontiguousarray(woT[h0 * 128:(h0 + 2) * 128, :]))
        return wq, wo

    wq0, wo0 = att_slices(wqkvT0, woT0)
    wq1, wo1 = att_slices(wqkvT1, woT1)
    bqk1, bv1 = [], []
    for c in CC:
        h0 = 2 * c
        cols = [b_attn[h0 * 128:(h0 + 1) * 128], b_attn[(h0 + 1) * 128:(h0 + 2) * 128],
                b_attn[2048 + h0 * 128:2048 + (h0 + 1) * 128],
                b_attn[2048 + (h0 + 1) * 128:2048 + (h0 + 2) * 128],
                np.zeros(128, np.float32), np.zeros(128, np.float32)]
        bqk1.append(np.ascontiguousarray(np.stack(cols, axis=1)))
        bv1.append(np.ascontiguousarray(b_attn[4096 + h0 * 128:4096 + (h0 + 2) * 128][None, :]))

    def run(name, maps):
        nc = _get_program(name)
        return run_bass_kernel_spmd(nc, maps, core_ids=CC).results

    maps = []
    for c in CC:
        m = {"x_in": np.ascontiguousarray(x0[c * T:(c + 1) * T].T), "conf": conf, "conb": conb}
        m.update(w0)
        maps.append(m)
    r1 = run("ffn0a", maps)
    x1 = [r1[c]["x_out"] for c in CC]
    lnfull = np.concatenate([r1[c]["ln_out"] for c in CC], axis=0)

    maps = [{"lnfull": lnfull, "wq": wq0[c], "wo": wo0[c], "conf": conf, "conb": conb}
            for c in CC]
    r2 = run("att0", maps)

    def parts_for(rres, c):
        return np.ascontiguousarray(np.stack(
            [rres[r]["contrib"][c * 2048:(c + 1) * 2048] for r in CC], axis=0))

    maps = []
    for c in CC:
        m = {"x_in": x1[c], "parts": parts_for(r2, c), "conf": conf, "conb": conb}
        m.update(w0)
        m.update(w1)
        maps.append(m)
    r3 = run("mid", maps)
    x4 = [r3[c]["x_out"] for c in CC]
    lnfull4 = np.concatenate([r3[c]["ln_out"] for c in CC], axis=0)

    maps = [{"lnfull": lnfull4, "wq": wq1[c], "wo": wo1[c], "bqk": bqk1[c], "bv": bv1[c],
             "conf": conf, "conb": conb} for c in CC]
    r4 = run("att1", maps)

    maps = []
    for c in CC:
        m = {"x_in": x4[c], "parts": parts_for(r4, c), "bias_pc": bias_pc,
             "conf": conf, "conb": conb}
        m.update(w1)
        maps.append(m)
    r5 = run("end", maps)

    out = np.concatenate([r5[c]["x_out"].T for c in CC], axis=0)
    return out[None, :, :].astype(np.float32)


def emit_resid_rs(b, ctx, ACC, rs_pair, bias_sb, deps, half_deps):
    last = None
    for hw in range(2):
        PB3 = cbf(ctx, 48 + hw * 8, 56 + hw * 8).rearrange("p (c t) -> p c t", t=512)
        ld = b.dma("y", PB3, rs_pair[hw].rearrange("(c p) t -> p c t", p=128),
                   waits=deps + [half_deps[hw]], sem=f"dpr{hw}")
        for c in range(16):
            last = b.emit("v", lambda e, c=c, hw=hw, PB3=PB3: e.tensor_tensor(
                ACC[:, c * 1024 + hw * 512:c * 1024 + hw * 512 + 512],
                ACC[:, c * 1024 + hw * 512:c * 1024 + hw * 512 + 512],
                PB3[:, c, :], ALU.add),
                waits=[ld] + ([last] if last else []) + deps, inc="v")
    if bias_sb is not None:
        for c in range(16):
            last = b.emit("v", lambda e, c=c: e.tensor_scalar_add(
                ACC[:, c * 1024:(c + 1) * 1024], ACC[:, c * 1024:(c + 1) * 1024],
                bias_sb[:, c:c + 1]), waits=[last], inc="v")
    return last


def build_fused_program():
    nc = bass.Bass("TRN2", target_bir_lowering=False)
    b = B(nc)
    x_in = nc.dram_tensor("x_in", [2048, 1024], F32, kind="ExternalInput").ap()
    conf_in = nc.dram_tensor("conf", [128, 1312], F32, kind="ExternalInput").ap()
    conb_in = nc.dram_tensor("conb", [128, 2177], BF16, kind="ExternalInput").ap()
    w0 = decl_ffn0_weights(nc)
    w1 = decl_ffn1_weights(nc)
    wq0_in = nc.dram_tensor("wq0", [2048, 768], F32, kind="ExternalInput").ap()
    wo0_in = nc.dram_tensor("wo0", [256, 2048], F32, kind="ExternalInput").ap()
    wq1_in = nc.dram_tensor("wq1", [2048, 768], F32, kind="ExternalInput").ap()
    wo1_in = nc.dram_tensor("wo1", [256, 2048], F32, kind="ExternalInput").ap()
    bqk_in = nc.dram_tensor("bqk", [128, 6], F32, kind="ExternalInput").ap()
    bv_in = nc.dram_tensor("bv", [1, 256], F32, kind="ExternalInput").ap()
    bias_pc = nc.dram_tensor("bias_pc", [128, 16], F32, kind="ExternalInput").ap()
    x_out = nc.dram_tensor("x_out", [2048, 1024], F32, kind="ExternalOutput").ap()

    xA = nc.dram_tensor("xA", [2048, 1024], F32, kind="Internal").ap()
    xB = nc.dram_tensor("xB", [2048, 1024], F32, kind="Internal").ap()
    xm = nc.dram_tensor("xm", [2048, 1024], F32, kind="Internal").ap()
    ln_c = [nc.dram_tensor(f"ln_c{i}", [2048, 512], BF16, kind="Internal") for i in range(2)]
    lnfull = [nc.dram_tensor(f"lnfull{i}", [16384, 512], BF16, kind="Internal",
                             addr_space="Shared") for i in range(2)]
    att_c = [nc.dram_tensor(f"att_c{i}", [16384, 512], BF16, kind="Internal") for i in range(2)]
    rs_o = [nc.dram_tensor(f"rs_o{i}", [2048, 512], BF16, kind="Internal") for i in range(2)]

    ctx = decl_common(nc, b)
    with nc.Block() as block:
        ACC = cf32(ctx, 16, 48)
        XLN = cbf(ctx, 0, 16)
        xb = cbf(ctx, 64, 68)
        r3 = lambda ap_: ap_.rearrange("(c p) t -> p c t", p=128)
        a3 = ACC.rearrange("p (c t) -> p c t", t=1024)
        l3 = XLN.rearrange("p (c t) -> p c t", t=1024)
        c_ev = emit_load_consts(b, ctx, conf_in, conb_in)
        bias_sb = ctx["SMALL"][:, 64:80]
        bp_ev = b.dma("y", bias_sb, bias_pc, waits=None, sem="dbp")

        def collective(kind, op, src, dst, waits):
            return b.emit("g", lambda e, src=src, dst=dst: e.collective_compute(
                kind, op, replica_groups=[list(range(NCORES))],
                ins=[src.ap().opt()], outs=[dst.ap().opt()]),
                waits=waits, inc="cc", inc_by=1)

        def attention_block(wq_in_, wo_in_, bias, xspill, ag_waits, wdeps):
            ags = []
            for hw in range(2):
                stC = b.dma("y", r3(ln_c[hw].ap()),
                            XLN.rearrange("p (c t) -> p c t", t=1024)[:, :, hw * 512:(hw + 1) * 512],
                            waits=ag_waits, sem=f"dsc{hw}")
                ags.append(collective("AllGather", ALU.bypass, ln_c[hw], lnfull[hw],
                                      [stC] + xspill))
            att = emit_attn(b, ctx, [t.ap() for t in lnfull], [t.ap() for t in att_c],
                            wq_in_, wo_in_, bias, deps=[ags[1]], wdeps=wdeps,
                            half_deps=ags)
            rss = []
            for hw in range(2):
                rss.append(collective("ReduceScatter", ALU.add, att_c[hw], rs_o[hw],
                                      att["half_stores"][hw]))
            return att, rss

        acc_ev = b.dma("y", a3, r3(x_in), sem="dx")
        ln0, lnch0 = emit_ln(b, ctx, ACC, XLN, xb, deps=[acc_ev] + c_ev)
        f0 = emit_ffn0(b, ctx, ACC, XLN, w0, ln0, acc_ev, lnch=lnch0)
        ln1, _ = emit_ln(b, ctx, ACC, XLN, xb, deps=[f0])
        stA = b.dma("y", r3(xA), a3, waits=[f0, ln1], sem="dsa")
        att0, rs0 = attention_block(wq0_in, wo0_in, None, [stA], [ln1], wdeps=[f0])
        acc2 = b.dma("y", a3, r3(xA), waits=[att0["at_all"]], sem="dx")
        rr = emit_resid_rs(b, ctx, ACC, [t.ap() for t in rs_o], None,
                           deps=[acc2, att0["outproj_mm"]], half_deps=rs0)
        ln2, lnchf2 = emit_ln(b, ctx, ACC, XLN, xb, deps=[rr])
        f0b = emit_ffn0(b, ctx, ACC, XLN, w0, ln2, rr, lnch=lnchf2,
                        pre_dep=att0["qkv_mm_last"], wd_dep=att0["at_all"])
        ln3, lnchf3 = emit_ln(b, ctx, ACC, XLN, xb, deps=[f0b])
        stm = b.dma("y", r3(xm), a3, waits=[f0b, ln3], sem="dsm")
        stf1 = emit_ffn1(b, ctx, XLN, w1, xm, xB, ln3, deps=[stm, ln3], lnch=lnchf3)
        acc3 = b.dma("y", a3, r3(xB), waits=stf1, sem="dx")
        ln4, _ = emit_ln(b, ctx, ACC, XLN, xb, deps=[acc3])
        stA2 = b.dma("y", r3(xA), a3, waits=[ln4], sem="dsa")
        att1, rs1 = attention_block(wq1_in, wo1_in, (bqk_in, bv_in), [stA2], [ln4], wdeps=[acc3])
        acc4 = b.dma("y", a3, r3(xA), waits=[att1["at_all"]], sem="dx")
        rr2 = emit_resid_rs(b, ctx, ACC, [t.ap() for t in rs_o], bias_sb,
                            deps=[acc4, att1["outproj_mm"], bp_ev], half_deps=rs1)
        ln5, lnchf5 = emit_ln(b, ctx, ACC, XLN, xb, deps=[rr2])
        f0c = rr2
        stm2 = b.dma("y", r3(xm), a3, waits=[rr2, ln5], sem="dsm")
        stf2 = emit_ffn1(b, ctx, XLN, w1, xm, x_out, ln5, deps=[stm2, ln5], lnch=lnchf5)
        b.emit("y", lambda e: e.nop(), waits=stf2)
        b.finalize(block)
    return nc


def kernel_fused(inp_dict):
    x0 = np.ascontiguousarray(np.asarray(inp_dict["L0_attn_hidden_states"],
                                         dtype=np.float32)[0])
    conf, conb = make_consts()
    CC = list(range(NCORES))
    w0 = {
        "Wr": np.ascontiguousarray(np.asarray(inp_dict["L0_ffn_Wr"], np.float32)),
        "Wsg": np.ascontiguousarray(np.asarray(inp_dict["L0_ffn_Wsg"], np.float32).reshape(16, 128).T),
        "Wg": np.ascontiguousarray(np.asarray(inp_dict["L0_ffn_Wg"], np.float32)),
        "Wu": np.ascontiguousarray(np.asarray(inp_dict["L0_ffn_Wu"], np.float32)),
        "Wd": np.ascontiguousarray(np.asarray(inp_dict["L0_ffn_Wd"], np.float32)),
        "Sg": np.ascontiguousarray(np.asarray(inp_dict["L0_ffn_Sg"], np.float32)),
        "Su": np.ascontiguousarray(np.asarray(inp_dict["L0_ffn_Su"], np.float32)),
        "Sd": np.ascontiguousarray(np.asarray(inp_dict["L0_ffn_Sd"], np.float32)),
    }
    w1 = {
        "W1T": np.ascontiguousarray(np.asarray(inp_dict["L1_ffn_W1"], np.float32).T),
        "W3T": np.ascontiguousarray(np.asarray(inp_dict["L1_ffn_W3"], np.float32).T),
        "W2T": np.ascontiguousarray(np.asarray(inp_dict["L1_ffn_W2"], np.float32).T),
    }
    wqkvT0 = np.ascontiguousarray(np.asarray(inp_dict["L0_attn_w_qkv"], np.float32).T)
    woT0 = np.ascontiguousarray(np.asarray(inp_dict["L0_attn_w_o"], np.float32).T)
    wqkvT1 = np.ascontiguousarray(np.asarray(inp_dict["L1_attn_w_attn"], np.float32))
    woT1 = np.ascontiguousarray(np.asarray(inp_dict["L1_attn_w_proj"], np.float32))
    b_attn = np.asarray(inp_dict["L1_attn_b_attn"], np.float32)
    b_proj = np.asarray(inp_dict["L1_attn_b_proj"], np.float32)
    bias_pc = np.ascontiguousarray(b_proj.reshape(16, 128).T)

    nc = build_fused_program()
    maps = []
    for c in CC:
        h0 = 2 * c
        wq0 = np.ascontiguousarray(np.concatenate(
            [wqkvT0[:, h0 * 128:(h0 + 2) * 128],
             wqkvT0[:, 2048 + h0 * 128:2048 + (h0 + 2) * 128],
             wqkvT0[:, 4096 + h0 * 128:4096 + (h0 + 2) * 128]], axis=1))
        wq1 = np.ascontiguousarray(np.concatenate(
            [wqkvT1[:, h0 * 128:(h0 + 2) * 128],
             wqkvT1[:, 2048 + h0 * 128:2048 + (h0 + 2) * 128],
             wqkvT1[:, 4096 + h0 * 128:4096 + (h0 + 2) * 128]], axis=1))
        cols = [b_attn[h0 * 128:(h0 + 1) * 128], b_attn[(h0 + 1) * 128:(h0 + 2) * 128],
                b_attn[2048 + h0 * 128:2048 + (h0 + 1) * 128],
                b_attn[2048 + (h0 + 1) * 128:2048 + (h0 + 2) * 128],
                np.zeros(128, np.float32), np.zeros(128, np.float32)]
        m = {"x_in": np.ascontiguousarray(x0[c * T:(c + 1) * T].T),
             "conf": conf, "conb": conb,
             "wq0": wq0, "wo0": np.ascontiguousarray(woT0[h0 * 128:(h0 + 2) * 128, :]),
             "wq1": wq1, "wo1": np.ascontiguousarray(woT1[h0 * 128:(h0 + 2) * 128, :]),
             "bqk": np.ascontiguousarray(np.stack(cols, axis=1)),
             "bv": np.ascontiguousarray(b_attn[4096 + h0 * 128:4096 + (h0 + 2) * 128][None, :]),
             "bias_pc": bias_pc}
        m.update(w0)
        m.update(w1)
        maps.append(m)
    res = run_bass_kernel_spmd(nc, maps, core_ids=CC).results
    out = np.concatenate([res[c]["x_out"].T for c in CC], axis=0)
    return out[None, :, :].astype(np.float32)


def kernel(**inputs):
    return kernel_fused(inputs)


# version 08 (final)
# speedup vs baseline: 1.0528x; 1.0528x over previous
import numpy as np
import ml_dtypes
import concourse.bass as bass
import concourse.mybir as mybir
from concourse.bass_utils import run_bass_kernel_spmd

F32 = mybir.dt.float32
BF16 = mybir.dt.bfloat16
AF = mybir.ActivationFunctionType
ALU = mybir.AluOpType
AX = mybir.AxisListType

NCORES = 8
T = 1024
H = 2048
QSC = 0.2973017787506803

ENGS = ("t", "v", "s", "y", "g")

from collections import defaultdict


class B:
    def __init__(self, nc):
        self.nc = nc
        self.q = {e: [] for e in ENGS}
        self.cnt = defaultdict(int)
        self.lw = {e: defaultdict(int) for e in ENGS}
        self.sems = {}
        self.bank = [None] * 8

    def sem(self, name):
        if name not in self.sems:
            self.sems[name] = self.nc.semaphore("sem_" + name).__enter__()
        return self.sems[name]

    def emit(self, eng, mk, waits=None, inc=None, inc_by=1):
        acts = []
        for w in (waits or []):
            if w is None:
                continue
            s, v = w
            if v > 0 and self.lw[eng][s] < v:
                self.lw[eng][s] = v
                acts.append((s, v))
        ev = None
        if inc is not None:
            self.sem(inc)
            self.cnt[inc] += inc_by
            ev = (inc, self.cnt[inc])

        def fn(e, acts=acts, mk=mk, inc=inc, inc_by=inc_by):
            for (s, v) in acts:
                e.wait_ge(self.sems[s], v)
            ins = mk(e)
            if inc is not None:
                ins.then_inc(self.sems[inc], inc_by)
        self.q[eng].append(fn)
        return ev

    def dma(self, eng, out, in_, waits=None, sem=None):
        if sem is None:
            sem = "dy" if eng == "y" else "dg"
        return self.emit(eng, lambda e, o=out, i=in_: e.dma_start(out=o, in_=i),
                         waits=waits, inc=sem, inc_by=16)

    def finalize(self, block):
        for name, attr in (("t", "tensor"), ("v", "vector"), ("s", "scalar"),
                           ("y", "sync"), ("g", "gpsimd")):
            def body(e, name=name):
                for fn in self.q[name]:
                    fn(e)
            getattr(block, attr)(body)


def _hadamard(n):
    h = np.array([[1.0]], np.float32)
    while h.shape[0] < n:
        h = np.block([[h, h], [h, -h]])
    return h


def make_consts():
    conf = np.zeros((128, 1312), np.float32)
    conf[:, 0:128] = np.eye(128, dtype=np.float32)
    conf[0, 128:256] = 1.0
    conf[0, 1280] = 1e-5
    for e in range(8):
        conf[e, 256 + e * 128:256 + (e + 1) * 128] = 1.0
    conb = np.zeros((128, 2177), np.float32)
    for kt in range(4):
        q = np.arange(512)[None, :]
        p = np.arange(128)[:, None]
        conb[:, kt * 512:(kt + 1) * 512] = (q >= kt * 128 + p)
    conb[:, 2048:2049] = 1.0
    conb[:, 2049:2177] = _hadamard(128)
    return conf, conb.astype(ml_dtypes.bfloat16)


def decl_common(nc, b):
    ctx = {"nc": nc, "b": b}
    ctx["AB"] = nc.sbuf_tensor("AB", [128, 42 * 1024], F32).__enter__()
    ctx["ABH"] = ctx["AB"].bitcast(BF16)
    ctx["ROWS"] = nc.sbuf_tensor("ROWS", [1, 4 * 1024], F32).__enter__()
    ctx["SMALL"] = nc.sbuf_tensor("SMALL", [128, 80], F32).__enter__()
    ctx["CONF"] = nc.sbuf_tensor("CONF", [128, 1312], F32).__enter__()
    ctx["CONB"] = nc.sbuf_tensor("CONB", [128, 2177], BF16).__enter__()
    ctx["CONRT"] = nc.sbuf_tensor("CONRT", [128, 160], BF16).__enter__()
    ctx["C_T"] = nc.sbuf_tensor("C_T", [8, 1024], F32).__enter__()
    ctx["PS"] = [nc.psum_tensor(f"PS{i}", [128, 512], F32).__enter__() for i in range(8)]
    return ctx


def cbf(ctx, u0, u1):
    return ctx["ABH"][:, u0 * 1024:u1 * 1024]


def cf32(ctx, u0, u1):
    assert u0 % 2 == 0 and u1 % 2 == 0
    return ctx["AB"][:, (u0 // 2) * 1024:(u1 // 2) * 1024]


def eps_ap(ctx):
    return ctx["CONF"][0:1, 1280:1281]


def ident(ctx):
    return ctx["CONF"][:, 0:128]


def ones_row(ctx):
    return ctx["CONF"][0:1, 128:256]


def sel8(ctx, e):
    return ctx["CONF"][0:8, 256 + e * 128:256 + (e + 1) * 128]


def ones_bf(ctx):
    return ctx["CONB"][:, 2048:2049]


def h128c(ctx):
    return ctx["CONB"][:, 2049:2177]


def dmask(ctx, kt):
    return ctx["CONB"][:, kt * 512:(kt + 1) * 512]


def emit_load_consts(b, ctx, conf_in, conb_in):
    b.dma("y", ctx["CONF"][:, :], conf_in, sem="dcon")
    e2 = b.dma("y", ctx["CONB"][:, :], conb_in, sem="dcon")
    return [e2]


def mm(b, bank_ps, bank_idx, mk, waits, start, inc=None):
    w = list(waits)
    if start and b.bank[bank_idx] is not None:
        w.append(b.bank[bank_idx])
    return b.emit("t", mk, waits=w, inc=inc)


def evac(b, eng, bank_idx, mk, waits, inc):
    ev = b.emit(eng, mk, waits=waits, inc=inc)
    b.bank[bank_idx] = ev
    return ev


def emit_ln(b, ctx, ACC, XLN, xb, deps):
    PS = ctx["PS"]
    ROWS = ctx["ROWS"]
    s_row = ROWS[0:1, 0:1024]
    m_row = s_row
    q_row = ROWS[0:1, 1024:2048]
    ex2 = q_row
    m2 = ROWS[0:1, 2048:3072]
    b_row = m2
    r_row = ROWS[0:1, 3072:4096]
    RB = cf32(ctx, 68, 70)
    BB = cf32(ctx, 70, 72)

    A = lambda c: ACC[:, c * 1024:(c + 1) * 1024]
    XL = lambda c: XLN[:, c * 1024:(c + 1) * 1024]
    XB = lambda c: xb[:, (c % 4) * 1024:((c % 4) + 1) * 1024]

    xb_ev = [None] * 16
    sq_ev = [None] * 16
    mm_ev = [None] * 16
    for c in range(16):
        war = [mm_ev[c - 4]] if c >= 4 else []
        xb_ev[c] = b.emit("v", lambda e, c=c: e.tensor_copy(XB(c), A(c)),
                          waits=deps + war, inc="v")
        sq_ev[c] = b.emit("v", lambda e, c=c: e.tensor_tensor(XL(c), XB(c), XB(c), ALU.mult),
                          waits=[xb_ev[c]], inc="v")
        for hh in range(2):
            mm(b, PS, hh, lambda e, c=c, hh=hh: e.matmul(
                PS[hh][0:1, :], ones_bf(ctx), XB(c)[:, hh * 512:(hh + 1) * 512],
                start=(c == 0), stop=(c == 15)), [xb_ev[c]], start=(c == 0))
        for hh in range(2):
            mm_ev[c] = mm(b, PS, 2 + hh, lambda e, c=c, hh=hh: e.matmul(
                PS[2 + hh][0:1, :], ones_bf(ctx), XL(c)[:, hh * 512:(hh + 1) * 512],
                start=(c == 0), stop=(c == 15)), [sq_ev[c]], start=(c == 0),
                inc="t")
    mm_all = mm_ev[15]
    e1 = evac(b, "v", 0, lambda e: e.tensor_copy(s_row[:, 0:512], PS[0][0:1, :]), [mm_all], "v")
    e2 = evac(b, "v", 1, lambda e: e.tensor_copy(s_row[:, 512:1024], PS[1][0:1, :]), [mm_all], "v")
    e3 = evac(b, "v", 2, lambda e: e.tensor_copy(q_row[:, 0:512], PS[2][0:1, :]), [mm_all], "v")
    e4 = evac(b, "v", 3, lambda e: e.tensor_copy(q_row[:, 512:1024], PS[3][0:1, :]), [mm_all], "v")
    ev = b.emit("v", lambda e: e.tensor_scalar_mul(m_row, s_row, 1.0 / H), waits=[e1, e2], inc="v")
    ev2 = b.emit("v", lambda e: e.tensor_scalar_mul(ex2, q_row, 1.0 / H), waits=[e3, e4], inc="v")
    ev3 = b.emit("v", lambda e: e.tensor_mul(m2, m_row, m_row), waits=[ev], inc="v")
    var_ev = b.emit("v", lambda e: e.tensor_sub(ex2, ex2, m2), waits=[ev2, ev3], inc="v")
    ln_e = b.emit("s", lambda e: e.activation(r_row, ex2, AF.Ln, bias=eps_ap(ctx)), waits=[var_ev], inc="s")
    exp_e = b.emit("s", lambda e: e.activation(r_row, r_row, AF.Exp, scale=-0.5), waits=[ln_e], inc="s")
    br_ev = b.emit("v", lambda e: e.scalar_tensor_tensor(b_row, m_row, -1.0, r_row, ALU.mult, ALU.mult),
                   waits=[exp_e], inc="v")
    bc = []
    for (row, dst, base, wev) in ((r_row, RB, 4, exp_e), (b_row, BB, 6, br_ev)):
        for hh in range(2):
            tev = mm(b, PS, base + hh, lambda e, row=row, base=base, hh=hh: e.matmul(
                PS[base + hh][:, :], ones_row(ctx), row[:, hh * 512:(hh + 1) * 512],
                start=True, stop=True), [wev], start=True, inc="t")
            bc.append(evac(b, "v", base + hh, lambda e, dst=dst, base=base, hh=hh: e.tensor_copy(
                dst[:, hh * 512:(hh + 1) * 512], PS[base + hh][:, :]), [tev], "v"))
    out_ev = None
    apply_evs = []
    for c in range(16):
        ev1 = b.emit("v", lambda e, c=c: e.tensor_tensor(XL(c), A(c), RB, ALU.mult),
                     waits=bc + [mm_all], inc="v")
        out_ev = b.emit("v", lambda e, c=c: e.tensor_tensor(XL(c), XL(c), BB, ALU.add),
                        waits=[ev1], inc="v")
        apply_evs.append(out_ev)
    return out_ev, apply_evs


def emit_ffn0(b, ctx, ACC, XLN, w, ln_ev, acc_ev, lnch=None, pre_dep=None, wd_dep=None):
    if lnch is None:
        lnch = [ln_ev] * 16
    LNW = lambda c: lnch[c]
    PS = ctx["PS"]
    ROWS = ctx["ROWS"]
    SMALL = ctx["SMALL"]
    XL = lambda c: XLN[:, c * 1024:(c + 1) * 1024]
    A16 = lambda c: ACC[:, c * 1024:(c + 1) * 1024]
    SCR = cbf(ctx, 48, 59)
    HT = lambda ft: SCR[:, ft * 1024:(ft + 1) * 1024]
    CB = cbf(ctx, 59, 60)
    GB = cbf(ctx, 60, 61)
    SIL = cbf(ctx, 61, 62)
    USC = cbf(ctx, 62, 63)
    WG = [cbf(ctx, 72, 76), cbf(ctx, 76, 80)]
    WU = [cbf(ctx, 64, 68), cbf(ctx, 68, 72)]
    WDb = [ctx["ABH"][:, 80 * 1024:80 * 1024 + 1408],
           ctx["ABH"][:, 80 * 1024 + 1408:80 * 1024 + 2816]]
    C_T = ctx["C_T"][:, :]
    g_row = ROWS[0:1, 1024:2048]
    lg = SMALL[:, 0:8]
    negmax = SMALL[:, 8:9]
    lsum = SMALL[:, 9:10]
    rsum = SMALL[:, 10:11]
    m1 = SMALL[:, 11:12]
    m2c = SMALL[:, 12:13]
    t1 = SMALL[:, 16:24]
    keep = SMALL[:, 24:32]
    probs = SMALL[:, 32:40]

    b.dma("g", ctx["CONRT"][:, 0:128].rearrange("p (c e) -> p c e", e=8),
          w["Wr"].rearrange("(c p) e -> p c e", p=128), sem="dwr")
    wsg_ev = b.dma("g", ctx["CONRT"][:, 128:144], w["Wsg"], sem="dwr")
    wr_ev = wsg_ev
    Wr_sb = lambda c: ctx["CONRT"][:, c * 8:(c + 1) * 8]
    Wsg_sb = lambda c: ctx["CONRT"][:, 128 + c:129 + c]

    ct_last = None
    for tt in range(8):
        mme = None
        for c in range(16):
            mme = mm(b, PS, 0, lambda e, c=c, tt=tt: e.matmul(
                PS[0][:, 0:8], XL(c)[:, tt * 128:(tt + 1) * 128], Wr_sb(c),
                start=(c == 0), stop=(c == 15)), [LNW(c), wr_ev],
                start=(c == 0), inc="t" if c == 15 else None)
        ev = evac(b, "v", 0, lambda e: e.tensor_copy(lg, PS[0][:, 0:8]), [mme], "v")
        ev = b.emit("v", lambda e: e.reduce_max(negmax, lg, axis=AX.X, negate=True), waits=[ev], inc="v")
        ev = b.emit("s", lambda e: e.activation(probs, lg, AF.Exp, bias=negmax[:, 0:1]), waits=[ev], inc="s")
        ev = b.emit("v", lambda e: e.reduce_sum(lsum, probs, axis=AX.X), waits=[ev], inc="v")
        ev = b.emit("v", lambda e: e.reciprocal(rsum, lsum), waits=[ev], inc="v")
        ev = b.emit("v", lambda e: e.tensor_scalar_mul(probs, probs, rsum[:, 0:1]), waits=[ev], inc="v")
        ev = b.emit("v", lambda e: e.reduce_max(m1, probs, axis=AX.X), waits=[ev], inc="v")
        ev = b.emit("v", lambda e: e.tensor_scalar(t1, probs, m1[:, 0:1], None, ALU.is_ge), waits=[ev], inc="v")
        ev = b.emit("v", lambda e: e.scalar_tensor_tensor(t1, t1, -1.0e30, probs, ALU.mult, ALU.add),
                    waits=[ev], inc="v")
        ev = b.emit("v", lambda e: e.reduce_max(m2c, t1, axis=AX.X), waits=[ev], inc="v")
        ev = b.emit("v", lambda e: e.tensor_scalar(keep, probs, m2c[:, 0:1], None, ALU.is_ge), waits=[ev], inc="v")
        ev = b.emit("v", lambda e: e.tensor_mul(probs, probs, keep), waits=[ev], inc="v")
        tev = mm(b, PS, 1, lambda e: e.transpose(PS[1][0:8, 0:128], probs, ident(ctx)),
                 [ev], start=True, inc="t")
        ct_last = evac(b, "v", 1, lambda e, tt=tt: e.tensor_copy(
            C_T[:, tt * 128:(tt + 1) * 128], PS[1][0:8, 0:128]), [tev], "v")

    gmm = [None, None]
    for hh in range(2):
        for c in range(16):
            gmm[hh] = mm(b, PS, 2 + hh, lambda e, c=c, hh=hh: e.matmul(
                PS[2 + hh][0:1, 0:512], Wsg_sb(c), XL(c)[:, hh * 512:(hh + 1) * 512],
                start=(c == 0), stop=(c == 15)), [LNW(c), wsg_ev], start=(c == 0),
                inc="t" if c == 15 else None)
    sg0 = evac(b, "s", 2, lambda e: e.activation(g_row[:, 0:512], PS[2][0:1, :], AF.Sigmoid), [gmm[0]], "s")
    sg1 = evac(b, "s", 3, lambda e: e.activation(g_row[:, 512:1024], PS[3][0:1, :], AF.Sigmoid), [gmm[1]], "s")
    gb_last = None
    for hh in range(2):
        tev = mm(b, PS, 4 + hh, lambda e, hh=hh: e.matmul(
            PS[4 + hh][:, :], ones_row(ctx), g_row[:, hh * 512:(hh + 1) * 512],
            start=True, stop=True), [sg0, sg1], start=True, inc="t")
        gb_last = evac(b, "s", 4 + hh, lambda e, hh=hh: e.activation(
            GB[:, hh * 512:(hh + 1) * 512], PS[4 + hh][:, :], AF.Copy, scale=0.5), [tev], "s")

    def wgu_src(kind, ei, fg):
        wdt = 256 if fg < 5 else 128
        if ei < 8:
            t_ = w["Wg"] if kind == "g" else w["Wu"]
            return t_[ei].rearrange("(c p) f -> p c f", p=128)[:, :, fg * 256:fg * 256 + wdt]
        t_ = w["Sg"] if kind == "g" else w["Su"]
        col0 = (ei - 8) * 1408 + fg * 256
        return t_.rearrange("(c p) f -> p c f", p=128)[:, :, col0:col0 + wdt]

    def wd_src(ei, ot):
        if ei < 8:
            return w["Wd"][ei].rearrange("(c p) o -> p c o", p=128)[:, :, ot * 128:(ot + 1) * 128]
        r0 = (ei - 8) * 1408
        return w["Sd"][r0:r0 + 1408].rearrange("(c p) o -> p c o", p=128)[:, :, ot * 128:(ot + 1) * 128]

    WG3 = [t.rearrange("p (c f) -> p c f", f=256) for t in WG]
    WU3 = [t.rearrange("p (c f) -> p c f", f=256) for t in WU]
    WD3 = [t.rearrange("p (c f) -> p c f", f=128) for t in WDb]

    NEXP = 12
    gu_seq = [(ei, fg) for ei in range(NEXP) for fg in range(6)]
    wd_seq = [(ei, ot) for ei in range(NEXP) for ot in range(16)]
    wg_ld = {}
    wd_ld = {}
    wg_war = {0: [pre_dep] if pre_dep else [], 1: [wd_dep or pre_dep] if (wd_dep or pre_dep) else []}
    wu_war = {0: [ln_ev, pre_dep], 1: [ln_ev, pre_dep]}
    wd_war = {0: [wd_dep] if wd_dep else [], 1: [wd_dep] if wd_dep else []}

    def prefetch_gu(i):
        if i >= len(gu_seq):
            return
        ei, fg = gu_seq[i]
        slot = i % 2
        wdt = 256 if fg < 5 else 128
        wg_ld[(ei, fg)] = (
            b.dma("g", WG[slot][:, 0:16 * wdt].rearrange("p (c f) -> p c f", f=wdt),
                  wgu_src("g", ei, fg), waits=wg_war[slot], sem=f"dwg{slot}"),
            b.dma("g", WU[slot][:, 0:16 * wdt].rearrange("p (c f) -> p c f", f=wdt),
                  wgu_src("u", ei, fg), waits=wu_war[slot], sem=f"dwu{slot}"))

    def prefetch_wd(i):
        if i >= len(wd_seq):
            return
        ei, ot = wd_seq[i]
        slot = i % 2
        wd_ld[(ei, ot)] = b.dma("g", WD3[slot], wd_src(ei, ot), waits=wd_war[slot], sem=f"dwd{slot}")

    prefetch_gu(0)
    prefetch_gu(1)
    prefetch_wd(0)
    prefetch_wd(1)

    h_wr = [None] * 11
    h_rd = [None] * 11
    sil_rd = None
    usc_rd = None
    acc_upd = [acc_ev]
    gi = 0
    di = 0
    for ei in range(NEXP):
        if ei < 8:
            cb_last = None
            for hh in range(2):
                tev = mm(b, PS, 6 + hh, lambda e, ei=ei, hh=hh: e.matmul(
                    PS[6 + hh][:, :], sel8(ctx, ei), C_T[:, hh * 512:(hh + 1) * 512],
                    start=True, stop=True), [ct_last], start=True, inc="t")
                cb_last = evac(b, "s", 6 + hh, lambda e, hh=hh: e.activation(
                    CB[:, hh * 512:(hh + 1) * 512], PS[6 + hh][:, :], AF.Copy, scale=0.5),
                    [tev] + ([usc_rd] if usc_rd else []), "s")
            cb_cur, cb_tile = cb_last, CB
        else:
            cb_cur, cb_tile = gb_last, GB

        for fg in range(6):
            ldg, ldu = wg_ld[gu_seq[gi]]
            slot = gi % 2
            wdt = 256 if fg < 5 else 128
            ntile = 2 if fg < 5 else 1
            last_gm = None
            last_um = None
            for sub in range(ntile):
                ft = fg * 2 + sub
                gm = [None, None]
                um = [None, None]
                for c in range(16):
                    for hh in range(2):
                        gm[hh] = mm(b, PS, hh, lambda e, c=c, hh=hh, slot=slot, wdt=wdt, sub=sub: e.matmul(
                            PS[hh][:, :], WG[slot][:, c * wdt + sub * 128:c * wdt + sub * 128 + 128],
                            XL(c)[:, hh * 512:(hh + 1) * 512],
                            start=(c == 0), stop=(c == 15)), [ldg, LNW(c)], start=(c == 0),
                            inc="t" if c == 15 else None)
                for c in range(16):
                    for hh in range(2):
                        um[hh] = mm(b, PS, 2 + hh, lambda e, c=c, hh=hh, slot=slot, wdt=wdt, sub=sub: e.matmul(
                            PS[2 + hh][:, :], WU[slot][:, c * wdt + sub * 128:c * wdt + sub * 128 + 128],
                            XL(c)[:, hh * 512:(hh + 1) * 512],
                            start=(c == 0), stop=(c == 15)), [ldu, LNW(c)], start=(c == 0),
                            inc="t" if c == 15 else None)
                last_gm, last_um = gm, um
                sv = []
                uv = []
                for hh in range(2):
                    sv.append(evac(b, "s", hh, lambda e, hh=hh: e.activation(
                        SIL[:, hh * 512:(hh + 1) * 512], PS[hh][:, :], AF.Silu),
                        [gm[hh]] + ([sil_rd] if sil_rd else []), "s"))
                    uv.append(evac(b, "v", 2 + hh, lambda e, hh=hh, cb_tile=cb_tile: e.tensor_tensor(
                        USC[:, hh * 512:(hh + 1) * 512], PS[2 + hh][:, :],
                        cb_tile[:, hh * 512:(hh + 1) * 512], ALU.mult),
                        [um[hh], cb_cur] + ([usc_rd] if usc_rd else []), "v"))
                hv = b.emit("v", lambda e, ft=ft: e.tensor_tensor(HT(ft), SIL, USC, ALU.mult),
                            waits=sv + uv + ([h_rd[ft]] if h_rd[ft] else []), inc="v")
                h_wr[ft] = hv
                sil_rd = hv
                usc_rd = hv
            wg_war[slot] = [last_gm[1]]
            wu_war[slot] = [last_um[1]]
            gi += 1
            prefetch_gu(gi + 1)

        for ot in range(16):
            ldd = wd_ld[wd_seq[di]]
            slot = di % 2
            dm = [None, None]
            for fc in range(11):
                for hh in range(2):
                    dm[hh] = mm(b, PS, 4 + hh, lambda e, fc=fc, hh=hh, slot=slot: e.matmul(
                        PS[4 + hh][:, :], WD3[slot][:, fc, :], HT(fc)[:, hh * 512:(hh + 1) * 512],
                        start=(fc == 0), stop=(fc == 10)), [ldd, h_wr[fc]], start=(fc == 0),
                        inc="t" if fc == 10 else None)
            for hh in range(2):
                e_acc = evac(b, "v", 4 + hh, lambda e, ot=ot, hh=hh: e.tensor_tensor(
                    A16(ot)[:, hh * 512:(hh + 1) * 512], A16(ot)[:, hh * 512:(hh + 1) * 512],
                    PS[4 + hh][:, :], ALU.add), [dm[hh]] + acc_upd, "v")
            acc_upd = [e_acc]
            wd_war[slot] = [e_acc]
            di += 1
            prefetch_wd(di + 1)
        for ft in range(11):
            h_rd[ft] = dm[1]
    return acc_upd[0]


def decl_ffn0_weights(nc, pre=""):
    w = {}
    w["Wr"] = nc.dram_tensor(pre + "Wr", [2048, 8], F32, kind="ExternalInput").ap()
    w["Wsg"] = nc.dram_tensor(pre + "Wsg", [128, 16], F32, kind="ExternalInput").ap()
    w["Wg"] = nc.dram_tensor(pre + "Wg", [8, 2048, 1408], F32, kind="ExternalInput").ap()
    w["Wu"] = nc.dram_tensor(pre + "Wu", [8, 2048, 1408], F32, kind="ExternalInput").ap()
    w["Wd"] = nc.dram_tensor(pre + "Wd", [8, 1408, 2048], F32, kind="ExternalInput").ap()
    w["Sg"] = nc.dram_tensor(pre + "Sg", [2048, 5632], F32, kind="ExternalInput").ap()
    w["Su"] = nc.dram_tensor(pre + "Su", [2048, 5632], F32, kind="ExternalInput").ap()
    w["Sd"] = nc.dram_tensor(pre + "Sd", [5632, 2048], F32, kind="ExternalInput").ap()
    return w


def build_ffn0_program():
    nc = bass.Bass("TRN2", target_bir_lowering=False)
    b = B(nc)
    x_in = nc.dram_tensor("x_in", [2048, 1024], F32, kind="ExternalInput").ap()
    conf_in = nc.dram_tensor("conf", [128, 1312], F32, kind="ExternalInput").ap()
    conb_in = nc.dram_tensor("conb", [128, 2177], BF16, kind="ExternalInput").ap()
    w = decl_ffn0_weights(nc)
    x_out = nc.dram_tensor("x_out", [2048, 1024], F32, kind="ExternalOutput").ap()
    ln_out = nc.dram_tensor("ln_out", [2048, 1024], BF16, kind="ExternalOutput").ap()

    ctx = decl_common(nc, b)
    with nc.Block() as block:
        ACC = cf32(ctx, 16, 48)
        XLN = cbf(ctx, 0, 16)
        xb = cbf(ctx, 64, 68)
        c_ev = emit_load_consts(b, ctx, conf_in, conb_in)
        acc_ev = b.dma("y", ACC.rearrange("p (c t) -> p c t", t=1024),
                       x_in.rearrange("(c p) t -> p c t", p=128), sem="dx")
        ln_ev, lnch = emit_ln(b, ctx, ACC, XLN, xb, deps=[acc_ev] + c_ev)
        fin = emit_ffn0(b, ctx, ACC, XLN, w, ln_ev, acc_ev, lnch=lnch)
        ln2, lnch2 = emit_ln(b, ctx, ACC, XLN, xb, deps=[fin])
        st1 = b.dma("y", x_out.rearrange("(c p) t -> p c t", p=128),
                    ACC.rearrange("p (c t) -> p c t", t=1024), waits=[fin], sem="dst")
        st2 = b.dma("y", ln_out.rearrange("(c p) t -> p c t", p=128),
                    XLN.rearrange("p (c t) -> p c t", t=1024), waits=[ln2], sem="dst")
        b.emit("y", lambda e: e.nop(), waits=[st1, st2])
        b.finalize(block)
    return nc


def emit_attn(b, ctx, lnfull, contrib, wq_in, wo_in, bias, deps, wdeps=None, half_deps=None):
    PS = ctx["PS"]
    ROWS = ctx["ROWS"]
    SMALL = ctx["SMALL"]
    Q = [cbf(ctx, 0, 8), cbf(ctx, 8, 16)]
    Kb = [cbf(ctx, 16, 24), cbf(ctx, 24, 32)]
    Vb = [cbf(ctx, 32, 40), cbf(ctx, 40, 48)]
    WQK = cbf(ctx, 48, 56)
    WV = cbf(ctx, 56, 60)
    XST = [cbf(ctx, 60, 68), cbf(ctx, 68, 76)]
    PT = [cbf(ctx, 76, 77)[:, 0:512], cbf(ctx, 76, 77)[:, 512:1024]]
    CST = [cbf(ctx, 78, 79)[:, 0:512], cbf(ctx, 78, 79)[:, 512:1024]]
    LINV = cf32(ctx, 80, 82)
    NS = ctx["ABH"][:, 82 * 1024:82 * 1024 + 512]
    BV = ctx["ABH"][:, 82 * 1024 + 512:82 * 1024 + 768]
    WQK4 = WQK.rearrange("p (j c f) -> p j c f", c=16, f=128)
    WV3 = WV.rearrange("p (c f) -> p c f", f=256)
    XST3 = [t.rearrange("p (c f) -> p c f", f=512) for t in XST]
    lnf = [t.rearrange("(r c p) t -> r p c t", p=128, c=16) for t in lnfull]
    bqk_sb = SMALL[:, 48:54]
    bqs_sb = SMALL[:, 54:60]
    lrow = ROWS[0:1, 0:512]
    nrm = ROWS[0:1, 512:1024]
    qmax = SMALL[0:1, 40:42]
    kmax = SMALL[0:1, 42:44]
    tmp1 = SMALL[0:1, 44:45]
    negM = SMALL[:, 32:34]

    if wdeps is None:
        wdeps = deps
    for j in range(4):
        b.dma("g", WQK4[:, j, :, :],
              wq_in[:, j * 128:(j + 1) * 128].rearrange("(c p) o -> p c o", p=128),
              waits=wdeps, sem="dwq")
    wv_ev = b.dma("g", WV3, wq_in[:, 512:768].rearrange("(c p) o -> p c o", p=128),
                  waits=wdeps, sem="dwq")
    wq_ev = [wv_ev] * 4
    bias_ev = []
    if bias is not None:
        bqk_in, bv_in = bias
        b.dma("y", bqk_sb, bqk_in, waits=wdeps, sem="dbi")
        bve = b.dma("y", ROWS[0:1, 1024:1280], bv_in, waits=wdeps, sem="dbi")
        bs = bve
        t1 = mm(b, PS, 7, lambda e: e.matmul(PS[7][:, 0:256], ones_row(ctx), ROWS[0:1, 1024:1280],
                                             start=True, stop=True), [bve], start=True, inc="t")
        bvev = evac(b, "v", 7, lambda e: e.tensor_copy(BV, PS[7][:, 0:256]), [t1], "v")
        bias_ev = [bs, bvev]

    stage_war = [None, None]
    ev_v_done = []
    nrm_acc = {}
    ns_war = None
    last_red = None
    qkv_mm_last = None
    for r in range(8):
        for tb in range(2):
            g = r * 2 + tb
            buf = g % 2
            hdep = [half_deps[tb]] if half_deps else deps
            st_ev = b.dma("y", XST3[buf], lnf[tb][r, :, :, :],
                          waits=([stage_war[buf]] if stage_war[buf] else []) + hdep,
                          sem=f"dxs{buf}")
            col0 = g * 512
            last_mm = None
            qk_evs = []
            for j in range(4):
                mme = None
                for c in range(16):
                    mme = mm(b, PS, j, lambda e, j=j, c=c, buf=buf: e.matmul(
                        PS[j][:, :], WQK4[:, j, c, :], XST3[buf][:, c, :],
                        start=(c == 0), stop=(c == 15)), [st_ev, wq_ev[j]], start=(c == 0),
                        inc="t" if c == 15 else None)
                dstt = (Q[j] if j < 2 else Kb[j - 2])[:, col0:col0 + 512]
                if bias is not None:
                    eee = evac(b, "v", j, lambda e, dstt=dstt, j=j: e.tensor_scalar(
                        dstt, PS[j][:, :], bqk_sb[:, j:j + 1], QSC, ALU.add, ALU.mult),
                        [mme] + bias_ev, "v")
                else:
                    eee = evac(b, "s", j, lambda e, dstt=dstt, j=j: e.activation(
                        dstt, PS[j][:, :], AF.Copy, scale=QSC), [mme], "s")
                qk_evs.append((j, dstt, eee))
                last_mm = mme
            for tt in range(4):
                mme = None
                for c in range(16):
                    mme = mm(b, PS, 4, lambda e, tt=tt, c=c, buf=buf: e.matmul(
                        PS[4][:, 0:256], XST3[buf][:, c, tt * 128:(tt + 1) * 128], WV3[:, c, :],
                        start=(c == 0), stop=(c == 15)), [st_ev, wv_ev], start=(c == 0),
                        inc="t" if c == 15 else None)
                gtt = g * 4 + tt
                for h in range(2):
                    if bias is not None:
                        eee = evac(b, "v", 4, lambda e, h=h, gtt=gtt: e.tensor_tensor(
                            Vb[h][:, gtt * 128:(gtt + 1) * 128], PS[4][:, h * 128:(h + 1) * 128],
                            BV[:, h * 128:(h + 1) * 128], ALU.add), [mme] + bias_ev, "v")
                    else:
                        eee = evac(b, "v", 4, lambda e, h=h, gtt=gtt: e.tensor_copy(
                            Vb[h][:, gtt * 128:(gtt + 1) * 128], PS[4][:, h * 128:(h + 1) * 128]),
                            [mme], "v")
                    ev_v_done.append(eee)
                last_mm = mme
            stage_war[buf] = last_mm
            qkv_mm_last = last_mm
            for (j, dstt, eee) in qk_evs:
                key = j
                dst = qmax[:, j:j + 1] if j < 2 else kmax[:, j - 2:j - 1]
                sq_ev = b.emit("s", lambda e, dstt=dstt: e.activation(
                    NS, dstt, AF.Square),
                    waits=[eee, ns_war], inc="s")
                t_ev = mm(b, PS, 6, lambda e: e.matmul(
                    PS[6][0:1, 0:512], ones_bf(ctx), NS, start=True, stop=True),
                    [sq_ev], start=True, inc="t")
                ns_war = t_ev
                r_ev = b.emit("v", lambda e: e.reduce_max(tmp1, PS[6][0:1, 0:512], axis=AX.X),
                              waits=[t_ev, last_red], inc="v")
                b.bank[6] = r_ev
                if key not in nrm_acc:
                    a_ev = b.emit("v", lambda e, dst=dst: e.tensor_copy(dst, tmp1),
                                  waits=[r_ev], inc="v")
                else:
                    a_ev = b.emit("v", lambda e, dst=dst: e.tensor_max(dst, dst, tmp1),
                                  waits=[r_ev, nrm_acc[key]], inc="v")
                nrm_acc[key] = a_ev
                last_red = a_ev
    qkv_done = ev_v_done[-1]
    nrm_ev = [nrm_acc[j] for j in range(4)]
    negm_ev = []
    for h in range(2):
        s_ev = b.emit("v", lambda e, h=h: e.tensor_add(tmp1, qmax[:, h:h + 1], kmax[:, h:h + 1]),
                      waits=nrm_ev, inc="v")
        s_ev = b.emit("v", lambda e: e.tensor_scalar_mul(tmp1, tmp1, -0.5), waits=[s_ev], inc="v")
        t_ev = mm(b, PS, 6, lambda e: e.matmul(PS[6][:, 0:1], ones_row(ctx), tmp1,
                                               start=True, stop=True), [s_ev], start=True, inc="t")
        negm_ev.append(evac(b, "v", 6, lambda e, h=h: e.tensor_copy(negM[:, h:h + 1], PS[6][:, 0:1]),
                            [t_ev], "v"))

    PT4 = [PT[0], PT[1],
           cbf(ctx, 78, 79)[:, 0:512], cbf(ctx, 78, 79)[:, 512:1024]]
    LSC = [NS, cbf(ctx, 79, 80)[:, 0:512]]
    lrow2 = [lrow, ROWS[0:1, 512:1024]]
    LINV2 = [LINV[:, 0:512], LINV[:, 512:1024]]
    pt_war4 = [None, None, None, None]
    lsc_war = [None, None]
    attn_done = []
    for h in range(2):
        for qi in range(16):
            nkt = 4 * (qi + 1)
            pvb = 4 if qi % 2 == 0 else 7
            pv_mm = None
            ex_evs = [None] * nkt
            for step in range(nkt + 2):
                ktg = step
                if ktg < nkt:
                    sl = ktg % 4
                    sc_mm = mm(b, PS, sl, lambda e, h=h, qi=qi, ktg=ktg, sl=sl: e.matmul(
                        PS[sl][:, :], Kb[h][:, ktg * 128:(ktg + 1) * 128],
                        Q[h][:, qi * 512:(qi + 1) * 512], start=True, stop=True,
                        skip_group_check=True),
                        [qkv_done] + negm_ev, start=True, inc="t")
                    ex_ev = evac(b, "s", sl, lambda e, sl=sl, h=h: e.activation(
                        PT4[sl], PS[sl][:, :], AF.Exp, bias=negM[:, h:h + 1]),
                        [sc_mm] + ([pt_war4[sl]] if pt_war4[sl] else []), "s")
                    if ktg // 4 == qi:
                        ex_ev = b.emit("v", lambda e, sl=sl, kt=ktg % 4: e.tensor_tensor(
                            PT4[sl], PT4[sl], dmask(ctx, kt), ALU.mult), waits=[ex_ev], inc="v")
                    ex_evs[ktg] = ex_ev
                kt2 = step - 2
                if kt2 >= 0:
                    sl2 = kt2 % 4
                    if kt2 % 2 == 1:
                        lb2 = (kt2 // 2) % 2
                        se_ = b.emit("v", lambda e, sl2=sl2, lb2=lb2: e.tensor_add(
                            LSC[lb2], PT4[sl2 - 1], PT4[sl2]),
                            waits=[ex_evs[kt2 - 1], ex_evs[kt2], lsc_war[lb2]], inc="v")
                        mm(b, PS, 5, lambda e, lb2=lb2, kt2=kt2, nkt=nkt: e.matmul(
                            PS[5][0:1, :], ones_bf(ctx), LSC[lb2],
                            start=(kt2 == 1), stop=(kt2 == nkt - 1), skip_group_check=True),
                            [se_], start=(kt2 == 1), inc=None)
                    pv_mm = mm(b, PS, pvb, lambda e, h=h, kt2=kt2, sl2=sl2, nkt=nkt, pvb=pvb: e.matmul(
                        PS[pvb][:, :], Vb[h][:, kt2 * 128:(kt2 + 1) * 128], PT4[sl2],
                        start=(kt2 == 0), stop=(kt2 == nkt - 1), skip_group_check=True),
                        [ex_evs[kt2]], start=(kt2 == 0), inc="t" if kt2 == nkt - 1 else None)
                    pt_war4[sl2] = pv_mm
                    if kt2 % 2 == 1:
                        lsc_war[(kt2 // 2) % 2] = pv_mm
            qp = qi % 2
            le = evac(b, "v", 5, lambda e, qp=qp: e.tensor_copy(lrow2[qp], PS[5][0:1, :]),
                      [pv_mm], "v")
            le = b.emit("v", lambda e, qp=qp: e.reciprocal(lrow2[qp], lrow2[qp]),
                        waits=[le], inc="v")
            t_ev = mm(b, PS, 6, lambda e, qp=qp: e.matmul(PS[6][:, :], ones_row(ctx), lrow2[qp],
                                                          start=True, stop=True),
                      [le], start=True, inc="t")
            lb = evac(b, "v", 6, lambda e, qp=qp: e.tensor_copy(LINV2[qp], PS[6][:, :]),
                      [t_ev], "v")
            ae = evac(b, "v", pvb, lambda e, h=h, qi=qi, qp=qp, pvb=pvb: e.tensor_tensor(
                Q[h][:, qi * 512:(qi + 1) * 512], PS[pvb][:, :], LINV2[qp], ALU.mult),
                [lb, pv_mm], "v")
            attn_done.append(ae)
    at_all = attn_done[-1]

    WO = cbf(ctx, 48, 52)
    WO3 = WO.rearrange("p (c o) -> p c o", o=2048)
    wo_ev = b.dma("g", WO3, wo_in.rearrange("(c p) o -> p c o", p=128),
                  waits=[qkv_mm_last], sem="dwq")
    cst_war = [None, None]
    half_stores = []
    mme = None
    for hw in range(2):
        coloff = hw * 512
        for rr in range(8):
            tb = rr * 2 + hw
            for ot in range(16):
                sl = ot % 2
                mme = None
                for hch in range(2):
                    mme = mm(b, PS, sl, lambda e, tb=tb, ot=ot, hch=hch, sl=sl: e.matmul(
                        PS[sl][:, :], WO3[:, hch, ot * 128:(ot + 1) * 128],
                        Q[hch][:, tb * 512:(tb + 1) * 512],
                        start=(hch == 0), stop=(hch == 1)), [wo_ev, at_all], start=(hch == 0),
                        inc="t" if hch == 1 else None)
                ce = evac(b, "v", sl, lambda e, sl=sl: e.tensor_copy(CST[sl], PS[sl][:, :]),
                          [mme] + ([cst_war[sl]] if cst_war[sl] else []), "v")
                se = b.dma("y", contrib[hw][rr * 2048 + ot * 128: rr * 2048 + (ot + 1) * 128, :],
                           CST[sl], waits=[ce], sem=f"dcs{sl}")
                cst_war[sl] = se
        half_stores.append([cst_war[0], cst_war[1]])
    return {"half_stores": half_stores, "stores": [cst_war[0], cst_war[1]], "at_all": at_all,
            "outproj_mm": mme, "qkv_mm_last": qkv_mm_last, "qkv_done": qkv_done}


def build_att_program(with_bias):
    nc = bass.Bass("TRN2", target_bir_lowering=False)
    b = B(nc)
    lnfull = nc.dram_tensor("lnfull", [16384, 1024], BF16, kind="ExternalInput").ap()
    wq_in = nc.dram_tensor("wq", [2048, 768], F32, kind="ExternalInput").ap()
    wo_in = nc.dram_tensor("wo", [256, 2048], F32, kind="ExternalInput").ap()
    conf_in = nc.dram_tensor("conf", [128, 1312], F32, kind="ExternalInput").ap()
    conb_in = nc.dram_tensor("conb", [128, 2177], BF16, kind="ExternalInput").ap()
    bias = None
    if with_bias:
        bqk_in = nc.dram_tensor("bqk", [128, 6], F32, kind="ExternalInput").ap()
        bv_in = nc.dram_tensor("bv", [1, 256], F32, kind="ExternalInput").ap()
        bias = (bqk_in, bv_in)
    contrib = nc.dram_tensor("contrib", [16384, 1024], BF16, kind="ExternalOutput").ap()
    ctx = decl_common(nc, b)
    with nc.Block() as block:
        c_ev = emit_load_consts(b, ctx, conf_in, conb_in)
        last = emit_attn(b, ctx, [lnfull[:, 0:512], lnfull[:, 512:1024]],
                         [contrib[:, 0:512], contrib[:, 512:1024]],
                         wq_in, wo_in, bias, deps=c_ev)
        b.emit("y", lambda e: e.nop(), waits=last["stores"])
        b.finalize(block)
    return nc


def emit_ffn1(b, ctx, XLN, w, x_in, x_out, ln_ev, deps, lnch=None, cache=None, cache_state=None):
    if lnch is None:
        lnch = [ln_ev] * 16
    LNW = lambda c: lnch[c]
    """w: dict W1T [2048,8192], W3T [2048,8192], W2T [8192,2048] dram f32.
    x_in/x_out: dram [2048, 1024] f32. Returns last x-store event."""
    PS = ctx["PS"]
    XL = lambda c: XLN[:, c * 1024:(c + 1) * 1024]
    HID = cbf(ctx, 16, 48)
    HT1 = lambda fc: HID[:, fc * 512:(fc + 1) * 512]
    W1b = [cbf(ctx, 48, 50), cbf(ctx, 50, 52)]
    W3b = [cbf(ctx, 52, 54), cbf(ctx, 54, 56)]
    W2b = [cbf(ctx, 56, 60), cbf(ctx, 60, 64), cbf(ctx, 68, 72), cbf(ctx, 72, 76)]
    SILB = ctx["ABH"][:, 82 * 1024:82 * 1024 + 512]
    XR = [cf32(ctx, 64, 66)[:, 0:512], cf32(ctx, 64, 66)[:, 512:1024]]
    W1b3 = [t.rearrange("p (c f) -> p c f", f=128) for t in W1b]
    W3b3 = [t.rearrange("p (c f) -> p c f", f=128) for t in W3b]
    W2b3 = [t.rearrange("p (c f) -> p c f", f=128) for t in W2b]
    w1s = w["W1T"].rearrange("(c p) f -> p c f", p=128)
    w3s = w["W3T"].rearrange("(c p) f -> p c f", p=128)
    w2s = w["W2T"].rearrange("(c p) o -> p c o", p=128)
    xi3 = x_in.rearrange("(c p) t -> p c t", p=128)
    xo3 = x_out.rearrange("(c p) t -> p c t", p=128)
    HSC = float(1.0 / np.sqrt(8192.0))

    st_last = None
    w1_war = {0: [], 1: []}
    w3_war = {0: [], 1: []}
    w2_war = {0: [], 1: [], 2: [], 3: []}
    ld1 = {}
    ld3 = {}
    ld2 = {}
    h_ev = [None] * 64
    sil_war = None
    xr_war = [None, None]
    for half in range(2):
        hc = slice(half * 512, (half + 1) * 512)

        def pf(j, half=half):
            if j >= 64:
                return
            sl = j % 2
            if cache is not None and cache_state["w13"]:
                ld1[(half, j)] = b.dma("y", W1b[sl], cache["w1"][j], waits=w1_war[sl] + deps,
                                       sem=f"dw1r{sl}")
                ld3[(half, j)] = b.dma("y", W3b[sl], cache["w3"][j], waits=w3_war[sl] + deps,
                                       sem=f"dw3r{sl}")
            else:
                e1 = b.dma("g", W1b3[sl], w1s[:, :, j * 128:(j + 1) * 128],
                           waits=w1_war[sl] + deps, sem=f"dw1{sl}")
                e3 = b.dma("g", W3b3[sl], w3s[:, :, j * 128:(j + 1) * 128],
                           waits=w3_war[sl] + deps, sem=f"dw3{sl}")
                if cache is not None:
                    c1 = b.dma("y", cache["w1"][j], W1b[sl], waits=[e1], sem=f"cw1{sl}")
                    c3 = b.dma("y", cache["w3"][j], W3b[sl], waits=[e3], sem=f"cw3{sl}")
                    e1, e3 = [e1, c1], [e3, c3]
                ld1[(half, j)] = e1 if isinstance(e1, tuple) else e1
                ld3[(half, j)] = e3

        def h128_and_bfly(j):
            pgh = 2 + (j % 2) * 4
            te = mm(b, PS, pgh, lambda e, j=j, pgh=pgh: e.matmul(
                PS[pgh][:, :], h128c(ctx), HT1(j), start=True, stop=True,
                skip_group_check=True), [h_ev[j]], start=True, inc="t")
            h_ev[j] = evac(b, "s", pgh, lambda e, j=j, pgh=pgh: e.activation(
                HT1(j), PS[pgh][:, :], AF.Copy, scale=HSC), [te], "s")
            for st in range(6):
                bs = 1 << (st + 1)
                d = 1 << st
                if (j + 1) % bs == 0:
                    base = j + 1 - bs
                    for off in range(d):
                        a_i = base + off
                        b_i = base + off + d
                        e1 = b.emit("v", lambda e, a_i=a_i, b_i=b_i: e.tensor_add(
                            HT1(a_i), HT1(a_i), HT1(b_i)),
                            waits=[h_ev[a_i], h_ev[b_i]], inc="v")
                        e2 = b.emit("v", lambda e, a_i=a_i, b_i=b_i: e.scalar_tensor_tensor(
                            HT1(b_i), HT1(b_i), -2.0, HT1(a_i), ALU.mult, ALU.add),
                            waits=[e1], inc="v")
                        h_ev[a_i] = e2
                        h_ev[b_i] = e2
        pf(0)
        pf(1)
        pending = None
        for j in range(64):
            sl = j % 2
            pg = (j % 2) * 4
            gm = None
            um = None
            for c in range(16):
                l1e = ld1[(half, j)]
                l1e = l1e if isinstance(l1e, list) else [l1e]
                gm = mm(b, PS, pg, lambda e, c=c, sl=sl, pg=pg, hc=hc: e.matmul(
                    PS[pg][:, :], W1b3[sl][:, c, :], XL(c)[:, hc],
                    start=(c == 0), stop=(c == 15)), [l1e[0], LNW(c)], start=(c == 0),
                    inc="t" if c == 15 else None)
            for c in range(16):
                l3e = ld3[(half, j)]
                l3e = l3e if isinstance(l3e, list) else [l3e]
                um = mm(b, PS, pg + 1, lambda e, c=c, sl=sl, pg=pg, hc=hc: e.matmul(
                    PS[pg + 1][:, :], W3b3[sl][:, c, :], XL(c)[:, hc],
                    start=(c == 0), stop=(c == 15)), [l3e[0], LNW(c)], start=(c == 0),
                    inc="t" if c == 15 else None)
            w1_war[sl] = [gm] + (l1e[1:] if len(l1e) > 1 else [])
            w3_war[sl] = [um] + (l3e[1:] if len(l3e) > 1 else [])
            sv = evac(b, "s", pg, lambda e, pg=pg: e.activation(SILB, PS[pg][:, :], AF.Silu),
                      [gm] + ([sil_war] if sil_war else []), "s")
            hv = evac(b, "v", pg + 1, lambda e, j=j, pg=pg: e.tensor_tensor(
                HT1(j), PS[pg + 1][:, :], SILB, ALU.mult),
                [um, sv] + ([h_ev[j]] if h_ev[j] else []), "v")
            sil_war = hv
            h_ev[j] = hv
            pf(j + 2)
            if pending is not None:
                h128_and_bfly(pending)
            pending = j
        h128_and_bfly(pending)
        if cache is not None:
            cache_state["w13"] = True
        def pf2(i, half=half):
            if i >= 32:
                return
            ot, hw = i // 2, i % 2
            sl = i % 4
            if cache is not None and cache_state["w2"]:
                ld2[(half, i)] = b.dma("y", W2b[sl], cache["w2"][i], waits=w2_war[sl] + deps,
                                       sem=f"dw2r{sl}")
            else:
                e2 = b.dma("g", W2b3[sl],
                           w2s[:, hw * 32:(hw + 1) * 32, ot * 128:(ot + 1) * 128],
                           waits=w2_war[sl] + deps, sem=f"dw2{sl}")
                if cache is not None:
                    c2 = b.dma("y", cache["w2"][i], W2b[sl], waits=[e2], sem=f"cw2{sl}")
                    e2 = [e2, c2]
                ld2[(half, i)] = e2
        for i in range(4):
            pf2(i)
        for ot in range(16):
            xsl = ot % 2
            pg3 = 3 + (ot % 2) * 4
            xe = b.dma("y", XR[xsl], xi3[:, ot, hc],
                       waits=([xr_war[xsl]] if xr_war[xsl] else [ln_ev]) + deps,
                       sem=f"dxr{xsl}")
            dm = None
            for i2 in (2 * ot, 2 * ot + 1):
                sl = i2 % 4
                hw = i2 % 2
                l2e = ld2[(half, i2)]
                l2e = l2e if isinstance(l2e, list) else [l2e]
                for k in range(32):
                    fc = hw * 32 + k
                    dm = mm(b, PS, pg3, lambda e, k=k, sl=sl, fc=fc, pg3=pg3: e.matmul(
                        PS[pg3][:, :], W2b3[sl][:, k, :], HT1(fc),
                        start=(fc == 0), stop=(fc == 63)), [l2e[0], h_ev[fc]],
                        start=(fc == 0), inc="t" if k == 31 else None)
                w2_war[sl] = [dm] + (l2e[1:] if len(l2e) > 1 else [])
                pf2(i2 + 4)
            ae = evac(b, "v", pg3, lambda e, xsl=xsl, pg3=pg3: e.scalar_tensor_tensor(
                XR[xsl], PS[pg3][:, :], 0.5, XR[xsl], ALU.mult, ALU.add), [dm, xe], "v")
            st_last = b.dma("y", xo3[:, ot, hc], XR[xsl], waits=[ae], sem=f"dxw{xsl}")
            xr_war[xsl] = st_last
        if cache is not None:
            cache_state["w2"] = True
    return [xr_war[0], xr_war[1]]


def decl_ffn1_weights(nc, pre=""):
    w = {}
    w["W1T"] = nc.dram_tensor(pre + "W1T", [2048, 8192], F32, kind="ExternalInput").ap()
    w["W3T"] = nc.dram_tensor(pre + "W3T", [2048, 8192], F32, kind="ExternalInput").ap()
    w["W2T"] = nc.dram_tensor(pre + "W2T", [8192, 2048], F32, kind="ExternalInput").ap()
    return w


def build_ffn1_program():
    nc = bass.Bass("TRN2", target_bir_lowering=False)
    b = B(nc)
    x_in = nc.dram_tensor("x_in", [2048, 1024], F32, kind="ExternalInput").ap()
    conf_in = nc.dram_tensor("conf", [128, 1312], F32, kind="ExternalInput").ap()
    conb_in = nc.dram_tensor("conb", [128, 2177], BF16, kind="ExternalInput").ap()
    w = decl_ffn1_weights(nc)
    x_out = nc.dram_tensor("x_out", [2048, 1024], F32, kind="ExternalOutput").ap()
    ctx = decl_common(nc, b)
    with nc.Block() as block:
        ACC = cf32(ctx, 16, 48)
        XLN = cbf(ctx, 0, 16)
        xb = cbf(ctx, 64, 68)
        c_ev = emit_load_consts(b, ctx, conf_in, conb_in)
        acc_ev = b.dma("y", ACC.rearrange("p (c t) -> p c t", t=1024),
                       x_in.rearrange("(c p) t -> p c t", p=128), sem="dx")
        ln_ev, lnch = emit_ln(b, ctx, ACC, XLN, xb, deps=[acc_ev] + c_ev)
        st = emit_ffn1(b, ctx, XLN, w, x_in, x_out, ln_ev, deps=[ln_ev], lnch=lnch)
        b.emit("y", lambda e: e.nop(), waits=st)
        b.finalize(block)
    return nc


def emit_resid_parts(b, ctx, ACC, parts, bias_pc, deps):
    PB3 = cbf(ctx, 48, 64).rearrange("p (c t) -> p c t", t=1024)
    A16 = lambda c: ACC[:, c * 1024:(c + 1) * 1024]
    bias_ev = []
    if bias_pc is not None:
        be = b.dma("y", ctx["SMALL"][:, 16:32], bias_pc, waits=deps, sem="dbp")
        bias_ev = [be]
    last = None
    for r in range(8):
        ld = b.dma("y", PB3, parts[r].rearrange("(c p) t -> p c t", p=128),
                   waits=deps + ([last] if last else []), sem="dpr")
        for c in range(16):
            last = b.emit("v", lambda e, c=c: e.tensor_tensor(
                A16(c), A16(c), PB3[:, c, :], ALU.add),
                waits=[ld] + ([last] if last else []) + deps, inc="v")
    if bias_pc is not None:
        for c in range(16):
            last = b.emit("v", lambda e, c=c: e.tensor_scalar_add(
                A16(c), A16(c), ctx["SMALL"][:, 16 + c:17 + c]),
                waits=bias_ev + [last], inc="v")
    return last


def build_mid_program(final):
    nc = bass.Bass("TRN2", target_bir_lowering=False)
    b = B(nc)
    x_in = nc.dram_tensor("x_in", [2048, 1024], F32, kind="ExternalInput").ap()
    parts = nc.dram_tensor("parts", [8, 2048, 1024], BF16, kind="ExternalInput").ap()
    conf_in = nc.dram_tensor("conf", [128, 1312], F32, kind="ExternalInput").ap()
    conb_in = nc.dram_tensor("conb", [128, 2177], BF16, kind="ExternalInput").ap()
    w1 = decl_ffn1_weights(nc)
    x_out = nc.dram_tensor("x_out", [2048, 1024], F32, kind="ExternalOutput").ap()
    xm = nc.dram_tensor("xm", [2048, 1024], F32, kind="Internal").ap()
    if final:
        bias_pc = nc.dram_tensor("bias_pc", [128, 16], F32, kind="ExternalInput").ap()
        w0 = None
        ln_out = None
    else:
        bias_pc = None
        w0 = decl_ffn0_weights(nc)
        ln_out = nc.dram_tensor("ln_out", [2048, 1024], BF16, kind="ExternalOutput").ap()

    ctx = decl_common(nc, b)
    with nc.Block() as block:
        ACC = cf32(ctx, 16, 48)
        XLN = cbf(ctx, 0, 16)
        xb = cbf(ctx, 64, 68)
        c_ev = emit_load_consts(b, ctx, conf_in, conb_in)
        acc_ev = b.dma("y", ACC.rearrange("p (c t) -> p c t", t=1024),
                       x_in.rearrange("(c p) t -> p c t", p=128), sem="dx")
        rp = emit_resid_parts(b, ctx, ACC, parts, bias_pc, deps=[acc_ev] + c_ev)
        ln1, lnch1 = emit_ln(b, ctx, ACC, XLN, xb, deps=[rp])
        if not final:
            fin0 = emit_ffn0(b, ctx, ACC, XLN, w0, ln1, rp, lnch=lnch1)
            ln2, lnch2b = emit_ln(b, ctx, ACC, XLN, xb, deps=[fin0])
        else:
            fin0 = rp
            ln2, lnch2b = ln1, lnch1
        stm = b.dma("y", xm.rearrange("(c p) t -> p c t", p=128),
                    ACC.rearrange("p (c t) -> p c t", t=1024), waits=[fin0, ln2], sem="dst")
        stf = emit_ffn1(b, ctx, XLN, w1, xm, x_out, ln2, deps=[stm, ln2], lnch=lnch2b)
        if not final:
            acc2 = b.dma("y", ACC.rearrange("p (c t) -> p c t", t=1024),
                         x_out.rearrange("(c p) t -> p c t", p=128), waits=stf, sem="dx")
            ln3, _l3 = emit_ln(b, ctx, ACC, XLN, xb, deps=[acc2])
            st2 = b.dma("y", ln_out.rearrange("(c p) t -> p c t", p=128),
                        XLN.rearrange("p (c t) -> p c t", t=1024), waits=[ln3], sem="dst")
            b.emit("y", lambda e: e.nop(), waits=[st2] + stf)
        else:
            b.emit("y", lambda e: e.nop(), waits=stf)
        b.finalize(block)
    return nc


_PROGRAMS = {}


def _get_program(name):
    if name not in _PROGRAMS:
        if name == "ffn0a":
            _PROGRAMS[name] = build_ffn0_program()
        elif name == "att0":
            _PROGRAMS[name] = build_att_program(with_bias=False)
        elif name == "att1":
            _PROGRAMS[name] = build_att_program(with_bias=True)
        elif name == "mid":
            _PROGRAMS[name] = build_mid_program(final=False)
        elif name == "end":
            _PROGRAMS[name] = build_mid_program(final=True)
    return _PROGRAMS[name]


def kernel_unfused(L0_attn_hidden_states, L0_attn_w_qkv, L0_attn_w_o,
           L0_ffn_Wr, L0_ffn_Wg, L0_ffn_Wu, L0_ffn_Wd,
           L0_ffn_Sg, L0_ffn_Su, L0_ffn_Sd, L0_ffn_Wsg,
           L1_attn_w_attn, L1_attn_b_attn, L1_attn_w_proj, L1_attn_b_proj,
           L1_ffn_W1, L1_ffn_W2, L1_ffn_W3):
    x0 = np.ascontiguousarray(np.asarray(L0_attn_hidden_states, dtype=np.float32)[0])
    conf, conb = make_consts()
    CC = list(range(NCORES))

    w0 = {
        "Wr": np.ascontiguousarray(np.asarray(L0_ffn_Wr, np.float32)),
        "Wsg": np.ascontiguousarray(np.asarray(L0_ffn_Wsg, np.float32).reshape(16, 128).T),
        "Wg": np.ascontiguousarray(np.asarray(L0_ffn_Wg, np.float32)),
        "Wu": np.ascontiguousarray(np.asarray(L0_ffn_Wu, np.float32)),
        "Wd": np.ascontiguousarray(np.asarray(L0_ffn_Wd, np.float32)),
        "Sg": np.ascontiguousarray(np.asarray(L0_ffn_Sg, np.float32)),
        "Su": np.ascontiguousarray(np.asarray(L0_ffn_Su, np.float32)),
        "Sd": np.ascontiguousarray(np.asarray(L0_ffn_Sd, np.float32)),
    }
    w1 = {
        "W1T": np.ascontiguousarray(np.asarray(L1_ffn_W1, np.float32).T),
        "W3T": np.ascontiguousarray(np.asarray(L1_ffn_W3, np.float32).T),
        "W2T": np.ascontiguousarray(np.asarray(L1_ffn_W2, np.float32).T),
    }
    wqkvT0 = np.ascontiguousarray(np.asarray(L0_attn_w_qkv, np.float32).T)
    woT0 = np.ascontiguousarray(np.asarray(L0_attn_w_o, np.float32).T)
    wqkvT1 = np.ascontiguousarray(np.asarray(L1_attn_w_attn, np.float32))
    woT1 = np.ascontiguousarray(np.asarray(L1_attn_w_proj, np.float32))
    b_attn = np.asarray(L1_attn_b_attn, np.float32)
    b_proj = np.asarray(L1_attn_b_proj, np.float32)
    bias_pc = np.ascontiguousarray(b_proj.reshape(16, 128).T)

    def att_slices(wqkvT, woT):
        wq, wo = [], []
        for c in CC:
            h0 = 2 * c
            wq.append(np.ascontiguousarray(np.concatenate(
                [wqkvT[:, h0 * 128:(h0 + 2) * 128],
                 wqkvT[:, 2048 + h0 * 128:2048 + (h0 + 2) * 128],
                 wqkvT[:, 4096 + h0 * 128:4096 + (h0 + 2) * 128]], axis=1)))
            wo.append(np.ascontiguousarray(woT[h0 * 128:(h0 + 2) * 128, :]))
        return wq, wo

    wq0, wo0 = att_slices(wqkvT0, woT0)
    wq1, wo1 = att_slices(wqkvT1, woT1)
    bqk1, bv1 = [], []
    for c in CC:
        h0 = 2 * c
        cols = [b_attn[h0 * 128:(h0 + 1) * 128], b_attn[(h0 + 1) * 128:(h0 + 2) * 128],
                b_attn[2048 + h0 * 128:2048 + (h0 + 1) * 128],
                b_attn[2048 + (h0 + 1) * 128:2048 + (h0 + 2) * 128],
                np.zeros(128, np.float32), np.zeros(128, np.float32)]
        bqk1.append(np.ascontiguousarray(np.stack(cols, axis=1)))
        bv1.append(np.ascontiguousarray(b_attn[4096 + h0 * 128:4096 + (h0 + 2) * 128][None, :]))

    def run(name, maps):
        nc = _get_program(name)
        return run_bass_kernel_spmd(nc, maps, core_ids=CC).results

    maps = []
    for c in CC:
        m = {"x_in": np.ascontiguousarray(x0[c * T:(c + 1) * T].T), "conf": conf, "conb": conb}
        m.update(w0)
        maps.append(m)
    r1 = run("ffn0a", maps)
    x1 = [r1[c]["x_out"] for c in CC]
    lnfull = np.concatenate([r1[c]["ln_out"] for c in CC], axis=0)

    maps = [{"lnfull": lnfull, "wq": wq0[c], "wo": wo0[c], "conf": conf, "conb": conb}
            for c in CC]
    r2 = run("att0", maps)

    def parts_for(rres, c):
        return np.ascontiguousarray(np.stack(
            [rres[r]["contrib"][c * 2048:(c + 1) * 2048] for r in CC], axis=0))

    maps = []
    for c in CC:
        m = {"x_in": x1[c], "parts": parts_for(r2, c), "conf": conf, "conb": conb}
        m.update(w0)
        m.update(w1)
        maps.append(m)
    r3 = run("mid", maps)
    x4 = [r3[c]["x_out"] for c in CC]
    lnfull4 = np.concatenate([r3[c]["ln_out"] for c in CC], axis=0)

    maps = [{"lnfull": lnfull4, "wq": wq1[c], "wo": wo1[c], "bqk": bqk1[c], "bv": bv1[c],
             "conf": conf, "conb": conb} for c in CC]
    r4 = run("att1", maps)

    maps = []
    for c in CC:
        m = {"x_in": x4[c], "parts": parts_for(r4, c), "bias_pc": bias_pc,
             "conf": conf, "conb": conb}
        m.update(w1)
        maps.append(m)
    r5 = run("end", maps)

    out = np.concatenate([r5[c]["x_out"].T for c in CC], axis=0)
    return out[None, :, :].astype(np.float32)


def emit_resid_rs(b, ctx, ACC, rs_pair, bias_sb, deps, half_deps):
    last = None
    for hw in range(2):
        PB3 = cbf(ctx, 48 + hw * 8, 56 + hw * 8).rearrange("p (c t) -> p c t", t=512)
        ld = b.dma("y", PB3, rs_pair[hw].rearrange("(c p) t -> p c t", p=128),
                   waits=deps + [half_deps[hw]], sem=f"dpr{hw}")
        for c in range(16):
            last = b.emit("v", lambda e, c=c, hw=hw, PB3=PB3: e.tensor_tensor(
                ACC[:, c * 1024 + hw * 512:c * 1024 + hw * 512 + 512],
                ACC[:, c * 1024 + hw * 512:c * 1024 + hw * 512 + 512],
                PB3[:, c, :], ALU.add),
                waits=[ld] + ([last] if last else []) + deps, inc="v")
    if bias_sb is not None:
        for c in range(16):
            last = b.emit("v", lambda e, c=c: e.tensor_scalar_add(
                ACC[:, c * 1024:(c + 1) * 1024], ACC[:, c * 1024:(c + 1) * 1024],
                bias_sb[:, c:c + 1]), waits=[last], inc="v")
    return last


def build_fused_program():
    nc = bass.Bass("TRN2", target_bir_lowering=False)
    b = B(nc)
    x_in = nc.dram_tensor("x_in", [2048, 1024], F32, kind="ExternalInput").ap()
    conf_in = nc.dram_tensor("conf", [128, 1312], F32, kind="ExternalInput").ap()
    conb_in = nc.dram_tensor("conb", [128, 2177], BF16, kind="ExternalInput").ap()
    w0 = decl_ffn0_weights(nc)
    w1 = decl_ffn1_weights(nc)
    wq0_in = nc.dram_tensor("wq0", [2048, 768], F32, kind="ExternalInput").ap()
    wo0_in = nc.dram_tensor("wo0", [256, 2048], F32, kind="ExternalInput").ap()
    wq1_in = nc.dram_tensor("wq1", [2048, 768], F32, kind="ExternalInput").ap()
    wo1_in = nc.dram_tensor("wo1", [256, 2048], F32, kind="ExternalInput").ap()
    bqk_in = nc.dram_tensor("bqk", [128, 6], F32, kind="ExternalInput").ap()
    bv_in = nc.dram_tensor("bv", [1, 256], F32, kind="ExternalInput").ap()
    bias_pc = nc.dram_tensor("bias_pc", [128, 16], F32, kind="ExternalInput").ap()
    x_out = nc.dram_tensor("x_out", [2048, 1024], F32, kind="ExternalOutput").ap()

    xA = nc.dram_tensor("xA", [2048, 1024], F32, kind="Internal").ap()
    xB = nc.dram_tensor("xB", [2048, 1024], F32, kind="Internal").ap()
    xm = nc.dram_tensor("xm", [2048, 1024], F32, kind="Internal").ap()
    w1cache = nc.dram_tensor("w1cache", [64, 128, 2048], BF16, kind="Internal").ap()
    w3cache = nc.dram_tensor("w3cache", [64, 128, 2048], BF16, kind="Internal").ap()
    w2cache = nc.dram_tensor("w2cache", [32, 128, 4096], BF16, kind="Internal").ap()
    wcache = {"w1": [w1cache[j] for j in range(64)],
              "w3": [w3cache[j] for j in range(64)],
              "w2": [w2cache[i] for i in range(32)]}
    wcache_state = {"w13": False, "w2": False}
    ln_c = [nc.dram_tensor(f"ln_c{i}", [2048, 512], BF16, kind="Internal") for i in range(2)]
    lnfull = [nc.dram_tensor(f"lnfull{i}", [16384, 512], BF16, kind="Internal",
                             addr_space="Shared") for i in range(2)]
    att_c = [nc.dram_tensor(f"att_c{i}", [16384, 512], BF16, kind="Internal") for i in range(2)]
    rs_o = [nc.dram_tensor(f"rs_o{i}", [2048, 512], BF16, kind="Internal") for i in range(2)]

    ctx = decl_common(nc, b)
    with nc.Block() as block:
        ACC = cf32(ctx, 16, 48)
        XLN = cbf(ctx, 0, 16)
        xb = cbf(ctx, 64, 68)
        r3 = lambda ap_: ap_.rearrange("(c p) t -> p c t", p=128)
        a3 = ACC.rearrange("p (c t) -> p c t", t=1024)
        l3 = XLN.rearrange("p (c t) -> p c t", t=1024)
        c_ev = emit_load_consts(b, ctx, conf_in, conb_in)
        bias_sb = ctx["SMALL"][:, 64:80]
        bp_ev = b.dma("y", bias_sb, bias_pc, waits=None, sem="dbp")

        def collective(kind, op, src, dst, waits):
            return b.emit("g", lambda e, src=src, dst=dst: e.collective_compute(
                kind, op, replica_groups=[list(range(NCORES))],
                ins=[src.ap().opt()], outs=[dst.ap().opt()]),
                waits=waits, inc="cc", inc_by=1)

        def attention_block(wq_in_, wo_in_, bias, xspill, ag_waits, wdeps):
            ags = []
            for hw in range(2):
                stC = b.dma("y", r3(ln_c[hw].ap()),
                            XLN.rearrange("p (c t) -> p c t", t=1024)[:, :, hw * 512:(hw + 1) * 512],
                            waits=ag_waits, sem=f"dsc{hw}")
                ags.append(collective("AllGather", ALU.bypass, ln_c[hw], lnfull[hw],
                                      [stC] + xspill))
            att = emit_attn(b, ctx, [t.ap() for t in lnfull], [t.ap() for t in att_c],
                            wq_in_, wo_in_, bias, deps=[ags[1]], wdeps=wdeps,
                            half_deps=ags)
            rss = []
            for hw in range(2):
                rss.append(collective("ReduceScatter", ALU.add, att_c[hw], rs_o[hw],
                                      att["half_stores"][hw]))
            return att, rss

        acc_ev = b.dma("y", a3, r3(x_in), sem="dx")
        ln0, lnch0 = emit_ln(b, ctx, ACC, XLN, xb, deps=[acc_ev] + c_ev)
        f0 = emit_ffn0(b, ctx, ACC, XLN, w0, ln0, acc_ev, lnch=lnch0)
        ln1, _ = emit_ln(b, ctx, ACC, XLN, xb, deps=[f0])
        stA = b.dma("y", r3(xA), a3, waits=[f0, ln1], sem="dsa")
        att0, rs0 = attention_block(wq0_in, wo0_in, None, [stA], [ln1], wdeps=[f0])
        acc2 = b.dma("y", a3, r3(xA), waits=[att0["at_all"]], sem="dx")
        rr = emit_resid_rs(b, ctx, ACC, [t.ap() for t in rs_o], None,
                           deps=[acc2, att0["outproj_mm"]], half_deps=rs0)
        ln2, lnchf2 = emit_ln(b, ctx, ACC, XLN, xb, deps=[rr])
        f0b = emit_ffn0(b, ctx, ACC, XLN, w0, ln2, rr, lnch=lnchf2,
                        pre_dep=att0["qkv_mm_last"], wd_dep=att0["at_all"])
        ln3, lnchf3 = emit_ln(b, ctx, ACC, XLN, xb, deps=[f0b])
        stm = b.dma("y", r3(xm), a3, waits=[f0b, ln3], sem="dsm")
        stf1 = emit_ffn1(b, ctx, XLN, w1, xm, xB, ln3, deps=[stm, ln3], lnch=lnchf3,
                         cache=wcache, cache_state=wcache_state)
        acc3 = b.dma("y", a3, r3(xB), waits=stf1, sem="dx")
        ln4, _ = emit_ln(b, ctx, ACC, XLN, xb, deps=[acc3])
        stA2 = b.dma("y", r3(xA), a3, waits=[ln4], sem="dsa")
        att1, rs1 = attention_block(wq1_in, wo1_in, (bqk_in, bv_in), [stA2], [ln4], wdeps=[acc3])
        acc4 = b.dma("y", a3, r3(xA), waits=[att1["at_all"]], sem="dx")
        rr2 = emit_resid_rs(b, ctx, ACC, [t.ap() for t in rs_o], bias_sb,
                            deps=[acc4, att1["outproj_mm"], bp_ev], half_deps=rs1)
        ln5, lnchf5 = emit_ln(b, ctx, ACC, XLN, xb, deps=[rr2])
        f0c = rr2
        stm2 = b.dma("y", r3(xm), a3, waits=[rr2, ln5], sem="dsm")
        stf2 = emit_ffn1(b, ctx, XLN, w1, xm, x_out, ln5, deps=[stm2, ln5], lnch=lnchf5,
                         cache=wcache, cache_state=wcache_state)
        b.emit("y", lambda e: e.nop(), waits=stf2)
        b.finalize(block)
    return nc


def kernel_fused(inp_dict):
    x0 = np.ascontiguousarray(np.asarray(inp_dict["L0_attn_hidden_states"],
                                         dtype=np.float32)[0])
    conf, conb = make_consts()
    CC = list(range(NCORES))
    w0 = {
        "Wr": np.ascontiguousarray(np.asarray(inp_dict["L0_ffn_Wr"], np.float32)),
        "Wsg": np.ascontiguousarray(np.asarray(inp_dict["L0_ffn_Wsg"], np.float32).reshape(16, 128).T),
        "Wg": np.ascontiguousarray(np.asarray(inp_dict["L0_ffn_Wg"], np.float32)),
        "Wu": np.ascontiguousarray(np.asarray(inp_dict["L0_ffn_Wu"], np.float32)),
        "Wd": np.ascontiguousarray(np.asarray(inp_dict["L0_ffn_Wd"], np.float32)),
        "Sg": np.ascontiguousarray(np.asarray(inp_dict["L0_ffn_Sg"], np.float32)),
        "Su": np.ascontiguousarray(np.asarray(inp_dict["L0_ffn_Su"], np.float32)),
        "Sd": np.ascontiguousarray(np.asarray(inp_dict["L0_ffn_Sd"], np.float32)),
    }
    w1 = {
        "W1T": np.ascontiguousarray(np.asarray(inp_dict["L1_ffn_W1"], np.float32).T),
        "W3T": np.ascontiguousarray(np.asarray(inp_dict["L1_ffn_W3"], np.float32).T),
        "W2T": np.ascontiguousarray(np.asarray(inp_dict["L1_ffn_W2"], np.float32).T),
    }
    wqkvT0 = np.ascontiguousarray(np.asarray(inp_dict["L0_attn_w_qkv"], np.float32).T)
    woT0 = np.ascontiguousarray(np.asarray(inp_dict["L0_attn_w_o"], np.float32).T)
    wqkvT1 = np.ascontiguousarray(np.asarray(inp_dict["L1_attn_w_attn"], np.float32))
    woT1 = np.ascontiguousarray(np.asarray(inp_dict["L1_attn_w_proj"], np.float32))
    b_attn = np.asarray(inp_dict["L1_attn_b_attn"], np.float32)
    b_proj = np.asarray(inp_dict["L1_attn_b_proj"], np.float32)
    bias_pc = np.ascontiguousarray(b_proj.reshape(16, 128).T)

    nc = build_fused_program()
    maps = []
    for c in CC:
        h0 = 2 * c
        wq0 = np.ascontiguousarray(np.concatenate(
            [wqkvT0[:, h0 * 128:(h0 + 2) * 128],
             wqkvT0[:, 2048 + h0 * 128:2048 + (h0 + 2) * 128],
             wqkvT0[:, 4096 + h0 * 128:4096 + (h0 + 2) * 128]], axis=1))
        wq1 = np.ascontiguousarray(np.concatenate(
            [wqkvT1[:, h0 * 128:(h0 + 2) * 128],
             wqkvT1[:, 2048 + h0 * 128:2048 + (h0 + 2) * 128],
             wqkvT1[:, 4096 + h0 * 128:4096 + (h0 + 2) * 128]], axis=1))
        cols = [b_attn[h0 * 128:(h0 + 1) * 128], b_attn[(h0 + 1) * 128:(h0 + 2) * 128],
                b_attn[2048 + h0 * 128:2048 + (h0 + 1) * 128],
                b_attn[2048 + (h0 + 1) * 128:2048 + (h0 + 2) * 128],
                np.zeros(128, np.float32), np.zeros(128, np.float32)]
        m = {"x_in": np.ascontiguousarray(x0[c * T:(c + 1) * T].T),
             "conf": conf, "conb": conb,
             "wq0": wq0, "wo0": np.ascontiguousarray(woT0[h0 * 128:(h0 + 2) * 128, :]),
             "wq1": wq1, "wo1": np.ascontiguousarray(woT1[h0 * 128:(h0 + 2) * 128, :]),
             "bqk": np.ascontiguousarray(np.stack(cols, axis=1)),
             "bv": np.ascontiguousarray(b_attn[4096 + h0 * 128:4096 + (h0 + 2) * 128][None, :]),
             "bias_pc": bias_pc}
        m.update(w0)
        m.update(w1)
        maps.append(m)
    res = run_bass_kernel_spmd(nc, maps, core_ids=CC).results
    out = np.concatenate([res[c]["x_out"].T for c in CC], axis=0)
    return out[None, :, :].astype(np.float32)


def kernel(**inputs):
    return kernel_fused(inputs)
